# version 03
# baseline (speedup 1.0000x reference)
import numpy as np
from contextlib import ExitStack

import concourse.bass as bass
import concourse.mybir as mybir
from concourse.bass_utils import run_bass_kernel_spmd

F32 = mybir.dt.float32
BF16 = mybir.dt.bfloat16
AF = mybir.ActivationFunctionType
ALU = mybir.AluOpType
AX = mybir.AxisListType

L, D, E, FF = 4096, 1024, 8, 4096
TRUNC = None
LAST_NOPS = None
NPASS = 4
NTC = 8
NCHUNK = NPASS * NTC
NKC = 8
NXSLOT = 9
EPS = 1e-5
ITEMP = 10.0

ENGS = ("sync", "scalar", "vector", "tensor", "gpsimd")


class Res:
    __slots__ = ("name", "ready", "readers")
    def __init__(self, name):
        self.name = name
        self.ready = None
        self.readers = {}

class Sched:
    def __init__(self):
        self.ops = []
        self.cnt = {}

    SEM_ENG = {"pe": "tensor", "dve": "vector", "act": "scalar"}

    def emit(self, eng, fn, reads=(), writes=(), extra=(), sem=None, inc_n=1):
        waits = {}
        def add(ev):
            if ev is None:
                return
            s, v, peng = ev
            if waits.get(s, -1) < v:
                waits[s] = v
        for r in reads:
            add(r.ready)
        for w in writes:
            add(w.ready)
            for s, v in w.readers.items():
                add((s, v, self.SEM_ENG.get(s)))
        for ev in extra:
            add(ev)
        self.ops.append((eng, fn, tuple(waits.items()), sem, inc_n))
        ev = None
        if sem is not None:
            self.cnt[sem] = self.cnt.get(sem, 0) + inc_n
            peng = eng if sem in ("pe", "dve", "act") else None
            ev = (sem, self.cnt[sem], peng)
        for r in reads:
            if ev is not None:
                s, v, _ = ev
                if r.readers.get(s, -1) < v:
                    r.readers[s] = v
        for w in writes:
            w.ready = ev
            w.readers = {}
        return ev


def build_nc():
    nc = bass.Bass("TRN2", target_bir_lowering=False)

    x_d = nc.dram_tensor("x", [L, D], F32, kind="ExternalInput").ap()
    wgv_d = nc.dram_tensor("wgv", [1, D], F32, kind="ExternalInput").ap()
    bg_d = nc.dram_tensor("bgs", [1], F32, kind="ExternalInput").ap()
    wout_d = nc.dram_tensor("wout", [D, D], F32, kind="ExternalInput").ap()
    bout_d = nc.dram_tensor("bout", [D], F32, kind="ExternalInput").ap()
    wr_d = nc.dram_tensor("wrouter", [D, E], F32, kind="ExternalInput").ap()
    wg8_d = nc.dram_tensor("wgating", [D, E], F32, kind="ExternalInput").ap()
    wgate_d = nc.dram_tensor("wgate", [E, D, FF], F32, kind="ExternalInput").ap()
    wup_d = nc.dram_tensor("wup", [E, D, FF], F32, kind="ExternalInput").ap()
    wdown_d = nc.dram_tensor("wdown", [E, FF, D], F32, kind="ExternalInput").ap()
    iden_d = nc.dram_tensor("iden", [128, 128], F32, kind="ExternalInput").ap()
    y_d = nc.dram_tensor("y", [L, D], F32, kind="ExternalOutput").ap()

    S = Sched()

    with ExitStack() as es:
        T = es.enter_context

        iden = T(nc.sbuf_tensor("iden_s", [128, 128], F32))
        ones1 = T(nc.sbuf_tensor("ones1", [1, 128], F32))
        onesr = T(nc.sbuf_tensor("onesr", [1, 512], F32))
        woutT = T(nc.sbuf_tensor("woutT", [128, 8192], BF16))
        wrt = T(nc.sbuf_tensor("wrt", [128, 64], F32))
        wgt = T(nc.sbuf_tensor("wgt", [128, 64], F32))
        wgcol = T(nc.sbuf_tensor("wgcol", [128, 8], F32))
        bgv = T(nc.sbuf_tensor("bgv", [1, 1], F32))
        nbg = T(nc.sbuf_tensor("nbg", [1, 1], F32))
        boutr = T(nc.sbuf_tensor("boutr", [1, 1024], F32))
        hstate = T(nc.sbuf_tensor("hstate", [128, 8], F32))
        epsc = T(nc.sbuf_tensor("epsc", [128, 1], F32))
        xr = [T(nc.sbuf_tensor(f"xr{i}", [128, 1024], F32)) for i in range(NXSLOT)]
        xt = [T(nc.sbuf_tensor(f"xt{d}", [128, 512], F32)) for d in range(8)]
        htp = [T(nc.sbuf_tensor(f"htp{d}", [128, 1024], BF16)) for d in range(8)]
        x1f = [T(nc.sbuf_tensor(f"x1f{k}", [128, 1024], BF16)) for k in range(8)]
        x1r = T(nc.sbuf_tensor("x1r", [128, 1024], F32))
        asb = T(nc.sbuf_tensor("asb", [128, 512], F32))
        bsb = [T(nc.sbuf_tensor(f"bsb{i}", [128, 512], F32)) for i in range(2)]
        scano = [T(nc.sbuf_tensor(f"scano{i}", [128, 512], F32)) for i in range(2)]
        sg = [T(nc.sbuf_tensor(f"sg{i}", [128, 512], F32)) for i in range(2)]
        rsb = [T(nc.sbuf_tensor(f"rsb{i}", [128, 1024], F32)) for i in range(2)]
        oacc = [T(nc.sbuf_tensor(f"oacc{t}", [128, 1024], F32)) for t in range(8)]
        cw = T(nc.sbuf_tensor("cw_s", [128, 64], F32))
        erow = T(nc.sbuf_tensor("erow", [1, 512], F32))
        grow = T(nc.sbuf_tensor("grow", [1, 512], F32))
        arow = T(nc.sbuf_tensor("arow", [1, 512], F32))
        cols = {n: T(nc.sbuf_tensor(f"c_{n}", [128, 1], F32))
                for n in ("scol", "qcol", "mcol", "m2", "vcol", "lv", "rstd",
                          "zmax", "rsum", "zmax2", "gsum", "csum", "crec")}
        p8 = {n: T(nc.sbuf_tensor(f"p8_{n}", [128, 8], F32))
              for n in ("prs", "pre", "prf", "pgs", "pge", "pgf")}
        wgr = [T(nc.sbuf_tensor(f"wgr{i}", [128, 2048], BF16)) for i in range(2)]
        wur = [T(nc.sbuf_tensor(f"wur{i}", [128, 2048], BF16)) for i in range(2)]
        wdr = [T(nc.sbuf_tensor(f"wdr{i}", [128, 4096], BF16)) for i in range(2)]

        TB = [T(nc.psum_tensor(f"tb{i}", [128, 512], F32)) for i in range(6)]
        TZf = T(nc.psum_tensor("tz", [128, 512], F32))
        TZ = TZf[0:1, :]
        TR = T(nc.psum_tensor("tr", [128, 16], F32))

        sem_names = (["pe", "dve", "act", "ms"]
                     + [f"xd{i}" for i in range(NXSLOT)]
                     + ["wgs0", "wgs1", "wus0", "wus1", "wds0", "wds1",
                        "oy0", "oy1"])
        sems = {n: T(nc.semaphore(n)) for n in sem_names}

        R = {}
        def res(name):
            if name not in R:
                R[name] = Res(name)
            return R[name]

        r_xr = [res(f"xr{i}") for i in range(NXSLOT)]
        r_xt = [res(f"xt{d}") for d in range(8)]
        r_htp = [res(f"htp{d}") for d in range(8)]
        r_x1f = [res(f"x1f{k}") for k in range(8)]
        r_x1r = res("x1r")
        r_asb = res("asb")
        r_bsb = [res(f"bsb{i}") for i in range(2)]
        r_scano = [res(f"scano{i}") for i in range(2)]
        r_sg = [res(f"sg{i}") for i in range(2)]
        r_rsb = [res(f"rsb{i}") for i in range(2)]
        r_oacc = [res(f"oacc{t}") for t in range(8)]
        r_cw = res("cw")
        r_rows = res("rows")
        r_cols = {n: res(f"c_{n}") for n in cols}
        r_p8 = {n: res(f"p8_{n}") for n in p8}
        r_TB = [res(f"tb{i}") for i in range(6)]
        r_TZ = res("tz")
        r_TR = res("tr")
        r_wgr = [res(f"wgr{i}") for i in range(2)]
        r_wur = [res(f"wur{i}") for i in range(2)]
        r_wdr = [res(f"wdr{i}") for i in range(2)]
        r_hst = res("hstate")
        r_const = res("const")
        r_woutT = res("woutT")

        fills = {"wgs": 0, "wus": 0, "wds": 0, "oy": 0,
                 **{f"xd{i}": 0 for i in range(NXSLOT)}}

        def dma(eng, sem_name, dst, src, reads=(), writes=(), **kw):
            fn = lambda e, dst=dst, src=src, kw=kw: e.dma_start(dst, src, **kw)
            return S.emit(eng, fn, reads=reads, writes=writes, sem=sem_name,
                          inc_n=16)

        dma("sync", "ms", iden[:], iden_d)
        dma("sync", "ms",
            wrt[:].rearrange("p (kc e) -> p kc e", kc=8),
            wr_d.rearrange("(kc p) e -> p kc e", p=128))
        dma("sync", "ms",
            wgt[:].rearrange("p (kc e) -> p kc e", kc=8),
            wg8_d.rearrange("(kc p) e -> p kc e", p=128))
        def slow_wg(e):
            return e.dma_start(wgcol[:], wgv_d[0, :].rearrange("(kc p) -> p kc", p=128),
                               allow_slow_non_contiguous=True)
        S.emit("sync", slow_wg, sem="ms", inc_n=16)
        dma("sync", "ms", bgv[:], bg_d.rearrange("(a b) -> a b", a=1))
        dma("sync", "ms", boutr[:], bout_d.rearrange("(a b) -> a b", a=1))
        for ec in range(8):
            dma("sync", "ms", oacc[ec][:], wout_d[ec*128:(ec+1)*128, :],
                writes=[r_oacc[ec]])
        EV_MS = ("ms", S.cnt["ms"], None)
        ms_events = [EV_MS]
        wout_stage_ev = [EV_MS] * 8
        for ec in range(8):
            r_oacc[ec].ready = EV_MS
        r_const.ready = EV_MS

        for c in range(min(NXSLOT, NCHUNK)):
            dma("sync", f"xd{c}", xr[c][:], x_d[c*128:(c+1)*128, :],
                writes=[r_xr[c]])
            fills[f"xd{c}"] += 1

        S.emit("vector", lambda e: e.memset(ones1[:], 1.0), writes=[r_const], sem="dve")
        S.emit("vector", lambda e: e.memset(onesr[:], 1.0), writes=[r_const], sem="dve")
        S.emit("vector", lambda e: e.memset(hstate[:], 0.0), writes=[r_hst], sem="dve")
        S.emit("vector", lambda e: e.memset(epsc[:], EPS), writes=[r_const], sem="dve")
        S.emit("vector",
               lambda e: e.tensor_scalar(nbg[:], bgv[:], -1.0, None, ALU.mult),
               extra=ms_events, writes=[r_const], sem="dve")

        for ec in range(8):
            for half in range(2):
                bank = r_TB[4 + half]
                def tfn(e, ec=ec, half=half):
                    last = None
                    for j in range(4):
                        dc = half * 4 + j
                        last = e.transpose(
                            TB[4 + half][:, j*128:(j+1)*128],
                            oacc[ec][:, dc*128:(dc+1)*128], iden[:])
                    return last
                S.emit("tensor", tfn, reads=[r_oacc[ec], r_const],
                       writes=[bank], extra=[wout_stage_ev[ec]] + ms_events,
                       sem="pe")
                def cfn(e, ec=ec, half=half):
                    last = None
                    for j in range(4):
                        dc = half * 4 + j
                        last = e.copy(woutT[:, dc*1024 + ec*128:
                                            dc*1024 + ec*128 + 128],
                                      TB[4 + half][:, j*128:(j+1)*128])
                    return last
                S.emit("scalar", cfn, reads=[bank], writes=[r_woutT], sem="act")

        def wg_src(n):
            idx = n % 128
            ee, s = idx // 16, idx % 16
            return wgate_d[ee, :, s*256:(s+1)*256].rearrange(
                "(kc p) f -> p kc f", p=128)
        def wu_src(n):
            idx = n % 128
            ee, s = idx // 16, idx % 16
            return wup_d[ee, :, s*256:(s+1)*256].rearrange(
                "(kc p) f -> p kc f", p=128)
        def wd_src(n):
            idx = n % 64
            ee, s8 = idx // 8, idx % 8
            return wdown_d[ee, s8*512:(s8+1)*512, :].rearrange(
                "(fc p) d -> p fc d", p=128)

        def emit_wg(n):
            sl = n % 2
            dma("gpsimd", f"wgs{sl}",
                wgr[sl][:].rearrange("p (kc f) -> p kc f", kc=8), wg_src(n),
                writes=[r_wgr[sl]])
            fills["wgs"] += 1
        def emit_wu(n):
            sl = n % 2
            dma("gpsimd", f"wus{sl}",
                wur[sl][:].rearrange("p (kc f) -> p kc f", kc=8), wu_src(n),
                writes=[r_wur[sl]])
            fills["wus"] += 1
        def emit_wd(n):
            sl = n % 2
            dma("gpsimd", f"wds{sl}",
                wdr[sl][:].rearrange("p (fc d) -> p fc d", fc=4), wd_src(n),
                writes=[r_wdr[sl]])
            fills["wds"] += 1

        emit_wg(0); emit_wu(0); emit_wd(0)
        emit_wg(1); emit_wu(1); emit_wd(1)
        nwg, nwu, nwd = 2, 2, 2

        def layer_norm(rslot, out_ap, out_res, read_res):
            rr = r_rsb[rslot]
            S.emit("vector",
                   lambda e: e.reduce_sum(cols["scol"][:], rsb[rslot][:], axis=AX.X),
                   reads=[rr], writes=[r_cols["scol"]], sem="dve")
            S.emit("vector",
                   lambda e: e.scalar_tensor_tensor(
                       x1r[:], rsb[rslot][:], 1.0, rsb[rslot][:],
                       ALU.mult, ALU.mult, accum_out=cols["qcol"][:]),
                   reads=[rr], writes=[r_x1r, r_cols["qcol"]], sem="dve")
            S.emit("vector",
                   lambda e: e.tensor_scalar(cols["mcol"][:], cols["scol"][:],
                                             1.0 / D, None, ALU.mult),
                   reads=[r_cols["scol"]], writes=[r_cols["mcol"]], sem="dve")
            S.emit("vector",
                   lambda e: e.tensor_tensor(cols["m2"][:], cols["mcol"][:],
                                             cols["mcol"][:], ALU.mult),
                   reads=[r_cols["mcol"]], writes=[r_cols["m2"]], sem="dve")
            S.emit("vector",
                   lambda e: e.scalar_tensor_tensor(
                       cols["vcol"][:], cols["qcol"][:], 1.0 / D, cols["m2"][:],
                       ALU.mult, ALU.subtract),
                   reads=[r_cols["qcol"], r_cols["m2"]],
                   writes=[r_cols["vcol"]], sem="dve")
            S.emit("scalar",
                   lambda e: e.activation(cols["lv"][:], cols["vcol"][:],
                                          AF.Ln, bias=epsc[:]),
                   reads=[r_cols["vcol"], r_const], writes=[r_cols["lv"]], sem="act")
            S.emit("scalar",
                   lambda e: e.activation(cols["rstd"][:], cols["lv"][:],
                                          AF.Exp, scale=-0.5),
                   reads=[r_cols["lv"]], writes=[r_cols["rstd"]], sem="act")
            return S.emit("vector",
                          lambda e: e.tensor_scalar(
                              out_ap, rsb[rslot][:], cols["mcol"][:],
                              cols["rstd"][:], ALU.subtract, ALU.mult),
                          reads=[rr, r_cols["mcol"], r_cols["rstd"]] + read_res,
                          writes=[out_res], sem="dve")

        for p in range(NPASS):
            for tg in range(2):
                for tc4 in range(4):
                    tc = tg * 4 + tc4
                    c = p * 8 + tc
                    slot = c % NXSLOT
                    for half in range(2):
                        bank = r_TB[half]
                        def tfn(e, slot=slot, half=half):
                            last = None
                            for j in range(4):
                                d = half * 4 + j
                                last = e.transpose(
                                    TB[half][:, j*128:(j+1)*128],
                                    xr[slot][:, d*128:(d+1)*128], iden[:])
                            return last
                        S.emit("tensor", tfn, reads=[r_xr[slot], r_const],
                               writes=[bank], sem="pe")
                        def efn(e, half=half, tc4=tc4):
                            last = None
                            for j in range(4):
                                d = half * 4 + j
                                last = e.tensor_copy(
                                    xt[d][:, tc4*128:(tc4+1)*128],
                                    TB[half][:, j*128:(j+1)*128])
                            return last
                        S.emit("vector", efn, reads=[bank],
                               writes=[r_xt[half*4], r_xt[half*4+1],
                                       r_xt[half*4+2], r_xt[half*4+3]],
                               sem="dve")
                def zfn(e):
                    last = None
                    for k in range(8):
                        last = e.matmul(TZ, wgcol[:, k:k+1], xt[k][:],
                                        start=(k == 0), stop=(k == 7))
                    return last
                S.emit("tensor", zfn, reads=[r_const] + r_xt, writes=[r_TZ],
                       sem="pe")
                S.emit("scalar",
                       lambda e: e.activation(erow[:], TZ, AF.Exp,
                                              bias=nbg[:], scale=-1.0),
                       reads=[r_TZ, r_const], writes=[r_rows], sem="act")
                S.emit("vector",
                       lambda e: e.tensor_scalar(grow[:], erow[:], 1.0, None,
                                                 ALU.add),
                       reads=[r_rows], writes=[r_rows], sem="dve")
                S.emit("vector", lambda e: e.reciprocal(grow[:], grow[:]),
                       reads=[r_rows], writes=[r_rows], sem="dve")
                S.emit("vector",
                       lambda e: e.tensor_scalar(arow[:], grow[:], -1.0, 1.0,
                                                 ALU.mult, ALU.add),
                       reads=[r_rows], writes=[r_rows], sem="dve")
                S.emit("tensor",
                       lambda e: e.matmul(TB[2][:], ones1[:], grow[:],
                                          start=True, stop=True),
                       reads=[r_rows, r_const], writes=[r_TB[2]], sem="pe")
                S.emit("tensor",
                       lambda e: e.matmul(TB[3][:], ones1[:], arow[:],
                                          start=True, stop=True),
                       reads=[r_rows, r_const], writes=[r_TB[3]], sem="pe")
                S.emit("vector", lambda e: e.tensor_copy(asb[:], TB[3][:]),
                       reads=[r_TB[3]], writes=[r_asb], sem="dve")
                for d in range(8):
                    par = d % 2
                    S.emit("vector",
                           lambda e, d=d, par=par: e.tensor_tensor(
                               bsb[par][:], xt[d][:], TB[2][:], ALU.mult),
                           reads=[r_xt[d], r_TB[2]], writes=[r_bsb[par]],
                           sem="dve")
                    S.emit("vector",
                           lambda e, d=d, par=par: e.tensor_tensor_scan(
                               scano[par][:], asb[:], bsb[par][:],
                               hstate[:, d:d+1], ALU.mult, ALU.add),
                           reads=[r_asb, r_bsb[par], r_hst],
                           writes=[r_scano[par]], sem="dve")
                    S.emit("vector",
                           lambda e, d=d, par=par: e.tensor_copy(
                               hstate[:, d:d+1], scano[par][:, 511:512]),
                           reads=[r_scano[par]], writes=[r_hst], sem="dve")
                    S.emit("scalar",
                           lambda e, d=d, par=par, tg=tg: e.copy(
                               htp[d][:, tg*512:(tg+1)*512], scano[par][:]),
                           reads=[r_scano[par]], writes=[r_htp[d]], sem="act")

            for tc in range(8):
                c = p * 8 + tc
                slot = c % NXSLOT
                rslot = tc % 2
                for eh in range(2):
                    bank = r_TB[eh]
                    def afn(e, tc=tc, eh=eh):
                        for dc in range(8):
                            e.matmul(TB[eh][:],
                                     htp[dc][:, tc*128:(tc+1)*128],
                                     woutT[:, dc*1024 + eh*512:
                                           dc*1024 + eh*512 + 512],
                                     start=(dc == 0), stop=False)
                        return e.matmul(TB[eh][:], ones1[:],
                                        boutr[:, eh*512:eh*512 + 512],
                                        start=False, stop=True)
                    S.emit("tensor", afn,
                           reads=r_htp + [r_woutT, r_const],
                           writes=[bank], sem="pe")
                    S.emit("vector",
                           lambda e, eh=eh, rslot=rslot, slot=slot:
                           e.tensor_tensor(rsb[rslot][:, eh*512:eh*512+512],
                                           TB[eh][:],
                                           xr[slot][:, eh*512:eh*512+512],
                                           ALU.add),
                           reads=[bank, r_xr[slot]], writes=[r_rsb[rslot]],
                           sem="dve")
                layer_norm(rslot, xr[slot][:], r_xr[slot], [])
                for half in range(2):
                    bank = r_TB[4 + half]
                    def tfn(e, slot=slot, half=half):
                        last = None
                        for j in range(4):
                            k = half * 4 + j
                            last = e.transpose(
                                TB[4 + half][:, j*128:(j+1)*128],
                                xr[slot][:, k*128:(k+1)*128], iden[:])
                        return last
                    S.emit("tensor", tfn, reads=[r_xr[slot], r_const],
                           writes=[bank], sem="pe")
                    def efn(e, half=half):
                        last = None
                        for j in range(4):
                            k = half * 4 + j
                            last = e.tensor_copy(
                                x1r[:, k*128:(k+1)*128],
                                TB[4 + half][:, j*128:(j+1)*128])
                        return last
                    evac_ev = S.emit("vector", efn, reads=[bank],
                                     writes=[r_x1r], sem="dve")
                    def cfn(e, half=half, tc=tc):
                        last = None
                        for j in range(4):
                            k = half * 4 + j
                            last = e.copy(
                                x1f[k][:, tc*128:(tc+1)*128],
                                TB[4 + half][:, j*128:(j+1)*128])
                        return last
                    S.emit("scalar", cfn, reads=[bank], extra=[evac_ev],
                           writes=[r_x1f[half*4], r_x1f[half*4+1],
                                   r_x1f[half*4+2], r_x1f[half*4+3]],
                           sem="act")
                def rfn(e):
                    last = None
                    for k in range(8):
                        last = e.matmul(TR[:, 0:8], x1r[:, k*128:(k+1)*128],
                                        wrt[:, k*8:(k+1)*8],
                                        start=(k == 0), stop=(k == 7))
                    for k in range(8):
                        last = e.matmul(TR[:, 8:16], x1r[:, k*128:(k+1)*128],
                                        wgt[:, k*8:(k+1)*8],
                                        start=(k == 0), stop=(k == 7))
                    return last
                S.emit("tensor", rfn, reads=[r_x1r, r_const], writes=[r_TR],
                       sem="pe")
                S.emit("vector",
                       lambda e: e.reduce_max(cols["zmax"][:], TR[:, 0:8],
                                              axis=AX.X),
                       reads=[r_TR], writes=[r_cols["zmax"]], sem="dve")
                S.emit("vector",
                       lambda e: e.tensor_scalar(p8["prs"][:], TR[:, 0:8],
                                                 cols["zmax"][:], None,
                                                 ALU.subtract),
                       reads=[r_TR, r_cols["zmax"]], writes=[r_p8["prs"]],
                       sem="dve")
                S.emit("scalar",
                       lambda e: e.activation(p8["pre"][:], p8["prs"][:], AF.Exp),
                       reads=[r_p8["prs"]], writes=[r_p8["pre"]], sem="act")
                S.emit("vector",
                       lambda e: e.reduce_sum(cols["rsum"][:], p8["pre"][:],
                                              axis=AX.X),
                       reads=[r_p8["pre"]], writes=[r_cols["rsum"]], sem="dve")
                S.emit("vector",
                       lambda e: e.reciprocal(cols["rsum"][:], cols["rsum"][:]),
                       reads=[r_cols["rsum"]], writes=[r_cols["rsum"]], sem="dve")
                S.emit("vector",
                       lambda e: e.tensor_scalar(p8["prf"][:], p8["pre"][:],
                                                 cols["rsum"][:], None, ALU.mult),
                       reads=[r_p8["pre"], r_cols["rsum"]],
                       writes=[r_p8["prf"]], sem="dve")
                S.emit("vector",
                       lambda e: e.reduce_max(cols["zmax2"][:], TR[:, 8:16],
                                              axis=AX.X),
                       reads=[r_TR], writes=[r_cols["zmax2"]], sem="dve")
                S.emit("vector",
                       lambda e: e.tensor_scalar(p8["pgs"][:], TR[:, 8:16],
                                                 cols["zmax2"][:], None,
                                                 ALU.subtract),
                       reads=[r_TR, r_cols["zmax2"]], writes=[r_p8["pgs"]],
                       sem="dve")
                S.emit("scalar",
                       lambda e: e.activation(p8["pge"][:], p8["pgs"][:],
                                              AF.Exp, scale=ITEMP),
                       reads=[r_p8["pgs"]], writes=[r_p8["pge"]], sem="act")
                S.emit("vector",
                       lambda e: e.reduce_sum(cols["gsum"][:], p8["pge"][:],
                                              axis=AX.X),
                       reads=[r_p8["pge"]], writes=[r_cols["gsum"]], sem="dve")
                S.emit("vector",
                       lambda e: e.reciprocal(cols["gsum"][:], cols["gsum"][:]),
                       reads=[r_cols["gsum"]], writes=[r_cols["gsum"]], sem="dve")
                S.emit("vector",
                       lambda e: e.tensor_scalar(p8["pgf"][:], p8["pge"][:],
                                                 cols["gsum"][:], None, ALU.mult),
                       reads=[r_p8["pge"], r_cols["gsum"]],
                       writes=[r_p8["pgf"]], sem="dve")
                S.emit("vector",
                       lambda e, tc=tc: e.tensor_tensor(
                           cw[:, tc*8:(tc+1)*8], p8["prf"][:], p8["pgf"][:],
                           ALU.mult),
                       reads=[r_p8["prf"], r_p8["pgf"]], writes=[r_cw],
                       sem="dve")
                S.emit("vector",
                       lambda e, tc=tc: e.reduce_sum(
                           cols["csum"][:], cw[:, tc*8:(tc+1)*8], axis=AX.X),
                       reads=[r_cw], writes=[r_cols["csum"]], sem="dve")
                S.emit("vector",
                       lambda e: e.tensor_scalar(cols["csum"][:], cols["csum"][:],
                                                 1e-9, None, ALU.add),
                       reads=[r_cols["csum"]], writes=[r_cols["csum"]], sem="dve")
                S.emit("vector",
                       lambda e: e.reciprocal(cols["crec"][:], cols["csum"][:]),
                       reads=[r_cols["csum"]], writes=[r_cols["crec"]], sem="dve")
                S.emit("vector",
                       lambda e, tc=tc: e.tensor_scalar(
                           cw[:, tc*8:(tc+1)*8], cw[:, tc*8:(tc+1)*8],
                           cols["crec"][:], None, ALU.mult),
                       reads=[r_cols["crec"]], writes=[r_cw], sem="dve")

            for ee in range(8):
                for q in range(4):
                    for fcq in range(8):
                        s16 = q * 4 + fcq // 2
                        wslot = (p * 128 + ee * 16 + s16) % 2
                        f_in = fcq % 2
                        for tg in range(2):
                            gpar = (fcq * 2 + tg) % 2
                            gbank, ubank = r_TB[gpar], r_TB[2 + gpar]
                            def gfn(e, wslot=wslot, f_in=f_in, tg=tg, gpar=gpar):
                                last = None
                                for k in range(8):
                                    last = e.matmul(
                                        TB[gpar][:],
                                        wgr[wslot][:, k*256 + f_in*128:
                                                    k*256 + f_in*128 + 128],
                                        x1f[k][:, tg*512:(tg+1)*512],
                                        start=(k == 0), stop=(k == 7))
                                return last
                            S.emit("tensor", gfn,
                                   reads=[r_wgr[wslot]] + r_x1f,
                                   writes=[gbank], sem="pe")
                            def ufn(e, wslot=wslot, f_in=f_in, tg=tg, gpar=gpar):
                                last = None
                                for k in range(8):
                                    last = e.matmul(
                                        TB[2 + gpar][:],
                                        wur[wslot][:, k*256 + f_in*128:
                                                    k*256 + f_in*128 + 128],
                                        x1f[k][:, tg*512:(tg+1)*512],
                                        start=(k == 0), stop=(k == 7))
                                return last
                            S.emit("tensor", ufn,
                                   reads=[r_wur[wslot]] + r_x1f,
                                   writes=[ubank], sem="pe")
                            S.emit("scalar",
                                   lambda e, gpar=gpar: e.activation(
                                       sg[gpar][:], TB[gpar][:], AF.Silu),
                                   reads=[gbank], writes=[r_sg[gpar]],
                                   sem="act")
                            S.emit("vector",
                                   lambda e, gpar=gpar, fcq=fcq, tg=tg:
                                   e.tensor_tensor(
                                       htp[fcq][:, tg*512:(tg+1)*512],
                                       sg[gpar][:], TB[2 + gpar][:], ALU.mult),
                                   reads=[r_sg[gpar], ubank],
                                   writes=[r_htp[fcq]], sem="dve")
                        if fcq % 2 == 1:
                            if nwg < NPASS * 128:
                                emit_wg(nwg); nwg += 1
                            if nwu < NPASS * 128:
                                emit_wu(nwu); nwu += 1
                    first = (ee == 0 and q == 0)
                    for tc in range(8):
                        for dh in range(2):
                            dpar = 4 + (tc * 2 + dh) % 2
                            bank = r_TB[dpar]
                            def dfn(e, tc=tc, dh=dh, dpar=dpar):
                                last = None
                                for fcq in range(8):
                                    wsl = fcq // 4
                                    last = e.matmul(
                                        TB[dpar][:],
                                        htp[fcq][:, tc*128:(tc+1)*128],
                                        wdr[wsl][:, (fcq % 4)*1024 + dh*512:
                                                 (fcq % 4)*1024 + dh*512 + 512],
                                        start=(fcq == 0), stop=(fcq == 7))
                                return last
                            S.emit("tensor", dfn,
                                   reads=r_htp + [r_wdr[0], r_wdr[1]],
                                   writes=[bank], sem="pe")
                            if first:
                                S.emit("vector",
                                       lambda e, tc=tc, dh=dh, dpar=dpar, ee=ee:
                                       e.tensor_scalar(
                                           oacc[tc][:, dh*512:dh*512+512],
                                           TB[dpar][:],
                                           cw[:, tc*8+ee:tc*8+ee+1], None,
                                           ALU.mult),
                                       reads=[bank, r_cw],
                                       writes=[r_oacc[tc]], sem="dve")
                            else:
                                S.emit("vector",
                                       lambda e, tc=tc, dh=dh, dpar=dpar, ee=ee:
                                       e.scalar_tensor_tensor(
                                           oacc[tc][:, dh*512:dh*512+512],
                                           TB[dpar][:],
                                           cw[:, tc*8+ee:tc*8+ee+1],
                                           oacc[tc][:, dh*512:dh*512+512],
                                           ALU.mult, ALU.add),
                                       reads=[bank, r_cw],
                                       writes=[r_oacc[tc]], sem="dve")
                    if nwd < NPASS * 64:
                        emit_wd(nwd); nwd += 1
                    if nwd < NPASS * 64:
                        emit_wd(nwd); nwd += 1

            for tc in range(8):
                c = p * 8 + tc
                slot = c % NXSLOT
                rslot = tc % 2
                for eh in range(2):
                    S.emit("vector",
                           lambda e, tc=tc, eh=eh, rslot=rslot, slot=slot:
                           e.tensor_tensor(
                               rsb[rslot][:, eh*512:eh*512+512],
                               oacc[tc][:, eh*512:eh*512+512],
                               xr[slot][:, eh*512:eh*512+512], ALU.add),
                           reads=[r_oacc[tc], r_xr[slot]],
                           writes=[r_rsb[rslot]], sem="dve")
                layer_norm(rslot, rsb[rslot][:], r_rsb[rslot], [])
                oyn = f"oy{rslot}"
                fills["oy"] += 1
                dma("sync", oyn, y_d[c*128:(c+1)*128, :], rsb[rslot][:],
                    reads=[r_rsb[rslot]])
                nxt = c + NXSLOT
                if nxt < NCHUNK:
                    dma("sync", f"xd{slot}", xr[slot][:],
                        x_d[nxt*128:(nxt+1)*128, :], writes=[r_xr[slot]])
                    fills[f"xd{slot}"] += 1

        S.emit("sync", lambda e: e.nop(),
               extra=[("oy0", S.cnt["oy0"], None), ("oy1", S.cnt["oy1"], None)])

        block = T(nc.Block())

        global LAST_NOPS
        LAST_NOPS = len(S.ops)
        ops_r = S.ops if TRUNC is None else S.ops[:TRUNC]
        def make_body(eng_name):
            def body(eng):
                floors = {}
                for op_eng, fn, waits, sem, inc_n in ops_r:
                    if op_eng != eng_name:
                        continue
                    for s, v in waits:
                        if floors.get(s, -1) < v:
                            eng.wait_ge(sems[s], v)
                            floors[s] = v
                    instr = fn(eng)
                    if sem is not None:
                        if instr is None:
                            instr = eng.nop()
                        instr.then_inc(sems[sem], inc_n)
            return body

        block.sync(make_body("sync"))
        block.scalar(make_body("scalar"))
        block.vector(make_body("vector"))
        block.tensor(make_body("tensor"))
        block.gpsimd(make_body("gpsimd"))

    return nc


_NC_CACHE = None

def kernel(L0_attn_x, L0_attn_Wg, L0_attn_bg, L0_attn_Wout, L0_attn_bout,
           L0_ffn_w_router, L0_ffn_w_gating, L0_ffn_w_gate, L0_ffn_w_up,
           L0_ffn_w_down):
    global _NC_CACHE
    if _NC_CACHE is None:
        _NC_CACHE = build_nc()
    nc = _NC_CACHE

    x = np.ascontiguousarray(np.asarray(L0_attn_x, dtype=np.float32))
    base = {
        "wgv": np.ascontiguousarray(np.asarray(L0_attn_Wg, dtype=np.float32)),
        "bgs": np.ascontiguousarray(np.asarray(L0_attn_bg, dtype=np.float32)),
        "wout": np.ascontiguousarray(np.asarray(L0_attn_Wout, dtype=np.float32)),
        "bout": np.ascontiguousarray(np.asarray(L0_attn_bout, dtype=np.float32)),
        "wrouter": np.ascontiguousarray(np.asarray(L0_ffn_w_router, dtype=np.float32)),
        "wgating": np.ascontiguousarray(np.asarray(L0_ffn_w_gating, dtype=np.float32)),
        "wgate": np.ascontiguousarray(np.asarray(L0_ffn_w_gate, dtype=np.float32)),
        "wup": np.ascontiguousarray(np.asarray(L0_ffn_w_up, dtype=np.float32)),
        "wdown": np.ascontiguousarray(np.asarray(L0_ffn_w_down, dtype=np.float32)),
        "iden": np.eye(128, dtype=np.float32),
    }
    in_maps = [dict(base, x=np.ascontiguousarray(x[b])) for b in range(8)]
    res = run_bass_kernel_spmd(nc, in_maps, core_ids=list(range(8)))
    return np.stack([np.asarray(r["y"]) for r in res.results], axis=0)


# version 04
# speedup vs baseline: 1.1005x; 1.1005x over previous
import numpy as np
from contextlib import ExitStack

import concourse.bass as bass
import concourse.mybir as mybir
from concourse.bass_utils import run_bass_kernel_spmd

F32 = mybir.dt.float32
BF16 = mybir.dt.bfloat16
AF = mybir.ActivationFunctionType
ALU = mybir.AluOpType
AX = mybir.AxisListType

L, D, E, FF = 4096, 1024, 8, 4096
TRUNC = None
LAST_NOPS = None
NPASS = 4
NTC = 8
NCHUNK = NPASS * NTC
NKC = 8
NXSLOT = 9
EPS = 1e-5
ITEMP = 10.0

ENGS = ("sync", "scalar", "vector", "tensor", "gpsimd")


class Res:
    __slots__ = ("name", "ready", "readers")
    def __init__(self, name):
        self.name = name
        self.ready = None
        self.readers = {}

class Sched:
    def __init__(self):
        self.ops = []
        self.cnt = {}

    SEM_ENG = {"pe": "tensor", "dve": "vector", "act": "scalar"}

    def emit(self, eng, fn, reads=(), writes=(), extra=(), sem=None, inc_n=1):
        waits = {}
        def add(ev):
            if ev is None:
                return
            s, v, peng = ev
            if waits.get(s, -1) < v:
                waits[s] = v
        for r in reads:
            add(r.ready)
        for w in writes:
            add(w.ready)
            for s, v in w.readers.items():
                add((s, v, self.SEM_ENG.get(s)))
        for ev in extra:
            add(ev)
        self.ops.append((eng, fn, tuple(waits.items()), sem, inc_n))
        ev = None
        if sem is not None:
            self.cnt[sem] = self.cnt.get(sem, 0) + inc_n
            peng = eng if sem in ("pe", "dve", "act") else None
            ev = (sem, self.cnt[sem], peng)
        for r in reads:
            if ev is not None:
                s, v, _ = ev
                if r.readers.get(s, -1) < v:
                    r.readers[s] = v
        for w in writes:
            w.ready = ev
            w.readers = {}
        return ev


def build_nc():
    nc = bass.Bass("TRN2", target_bir_lowering=False)

    x_d = nc.dram_tensor("x", [L, D], F32, kind="ExternalInput").ap()
    wgv_d = nc.dram_tensor("wgv", [1, D], F32, kind="ExternalInput").ap()
    bg_d = nc.dram_tensor("bgs", [1], F32, kind="ExternalInput").ap()
    wout_d = nc.dram_tensor("wout", [D, D], F32, kind="ExternalInput").ap()
    bout_d = nc.dram_tensor("bout", [D], F32, kind="ExternalInput").ap()
    wr_d = nc.dram_tensor("wrouter", [D, E], F32, kind="ExternalInput").ap()
    wg8_d = nc.dram_tensor("wgating", [D, E], F32, kind="ExternalInput").ap()
    wgate_d = nc.dram_tensor("wgate", [E, D, FF], F32, kind="ExternalInput").ap()
    wup_d = nc.dram_tensor("wup", [E, D, FF], F32, kind="ExternalInput").ap()
    wdown_d = nc.dram_tensor("wdown", [E, FF, D], F32, kind="ExternalInput").ap()
    iden_d = nc.dram_tensor("iden", [128, 128], F32, kind="ExternalInput").ap()
    y_d = nc.dram_tensor("y", [L, D], F32, kind="ExternalOutput").ap()

    S = Sched()

    with ExitStack() as es:
        T = es.enter_context

        iden = T(nc.sbuf_tensor("iden_s", [128, 128], F32))
        ones1 = T(nc.sbuf_tensor("ones1", [1, 128], F32))
        onesr = T(nc.sbuf_tensor("onesr", [1, 512], F32))
        woutT = T(nc.sbuf_tensor("woutT", [128, 8192], BF16))
        wrg = T(nc.sbuf_tensor("wrg", [128, 128], F32))
        wgcol = T(nc.sbuf_tensor("wgcol", [128, 8], F32))
        bgv = T(nc.sbuf_tensor("bgv", [1, 1], F32))
        nbg = T(nc.sbuf_tensor("nbg", [1, 1], F32))
        boutr = T(nc.sbuf_tensor("boutr", [1, 1024], F32))
        hstate = T(nc.sbuf_tensor("hstate", [128, 8], F32))
        epsc = T(nc.sbuf_tensor("epsc", [128, 1], F32))
        xr = [T(nc.sbuf_tensor(f"xr{i}", [128, 1024], F32)) for i in range(NXSLOT)]
        xt = [T(nc.sbuf_tensor(f"xt{d}", [128, 512], F32)) for d in range(8)]
        htp = [T(nc.sbuf_tensor(f"htp{d}", [128, 1024], BF16)) for d in range(8)]
        x1f = [T(nc.sbuf_tensor(f"x1f{k}", [128, 1024], BF16)) for k in range(8)]
        x1r = T(nc.sbuf_tensor("x1r", [128, 1024], F32))
        asb = T(nc.sbuf_tensor("asb", [128, 512], F32))
        bsb = [T(nc.sbuf_tensor(f"bsb{i}", [128, 512], F32)) for i in range(2)]
        scano = [T(nc.sbuf_tensor(f"scano{i}", [128, 512], F32)) for i in range(2)]
        sg = [T(nc.sbuf_tensor(f"sg{i}", [128, 512], F32)) for i in range(2)]
        rsb = [T(nc.sbuf_tensor(f"rsb{i}", [128, 1024], F32)) for i in range(2)]
        oacc = [T(nc.sbuf_tensor(f"oacc{t}", [128, 1024], F32)) for t in range(8)]
        cw = T(nc.sbuf_tensor("cw_s", [128, 64], F32))
        erow = T(nc.sbuf_tensor("erow", [1, 512], F32))
        grow = T(nc.sbuf_tensor("grow", [1, 512], F32))
        arow = T(nc.sbuf_tensor("arow", [1, 512], F32))
        cols = {n: T(nc.sbuf_tensor(f"c_{n}", [128, 1], F32))
                for n in ("scol", "qcol", "mcol", "m2", "vcol", "lv", "rstd",
                          "zmax", "rsum", "zmax2", "gsum", "csum", "crec")}
        p8 = {n: T(nc.sbuf_tensor(f"p8_{n}", [128, 8], F32))
              for n in ("prs", "pre", "prf", "pgs", "pge", "pgf")}
        wgr = [T(nc.sbuf_tensor(f"wgr{i}", [128, 2048], BF16)) for i in range(2)]
        wur = [T(nc.sbuf_tensor(f"wur{i}", [128, 2048], BF16)) for i in range(2)]
        wdr = [T(nc.sbuf_tensor(f"wdr{i}", [128, 4096], BF16)) for i in range(2)]

        TB = [T(nc.psum_tensor(f"tb{i}", [128, 512], F32)) for i in range(6)]
        TZf = T(nc.psum_tensor("tz", [128, 512], F32))
        TZ = TZf[0:1, :]
        TR = T(nc.psum_tensor("tr", [128, 16], F32))

        sem_names = (["pe", "dve", "act", "ms"]
                     + [f"xd{i}" for i in range(NXSLOT)]
                     + ["wgs0", "wgs1", "wus0", "wus1", "wds0", "wds1",
                        "oy0", "oy1"])
        sems = {n: T(nc.semaphore(n)) for n in sem_names}

        R = {}
        def res(name):
            if name not in R:
                R[name] = Res(name)
            return R[name]

        r_xr = [res(f"xr{i}") for i in range(NXSLOT)]
        r_xt = [res(f"xt{d}") for d in range(8)]
        r_htp = [res(f"htp{d}") for d in range(8)]
        r_x1f = [res(f"x1f{k}") for k in range(8)]
        r_x1r = res("x1r")
        r_asb = res("asb")
        r_bsb = [res(f"bsb{i}") for i in range(2)]
        r_scano = [res(f"scano{i}") for i in range(2)]
        r_sg = [res(f"sg{i}") for i in range(2)]
        r_rsb = [res(f"rsb{i}") for i in range(2)]
        r_oacc = [res(f"oacc{t}") for t in range(8)]
        r_cw = res("cw")
        r_rows = res("rows")
        r_cols = {n: res(f"c_{n}") for n in cols}
        r_p8 = {n: res(f"p8_{n}") for n in p8}
        r_TB = [res(f"tb{i}") for i in range(6)]
        r_TZ = res("tz")
        r_TR = res("tr")
        r_wgr = [res(f"wgr{i}") for i in range(2)]
        r_wur = [res(f"wur{i}") for i in range(2)]
        r_wdr = [res(f"wdr{i}") for i in range(2)]
        r_hst = res("hstate")
        r_const = res("const")
        r_woutT = res("woutT")

        fills = {"wgs": 0, "wus": 0, "wds": 0, "oy": 0,
                 **{f"xd{i}": 0 for i in range(NXSLOT)}}

        def dma(eng, sem_name, dst, src, reads=(), writes=(), **kw):
            fn = lambda e, dst=dst, src=src, kw=kw: e.dma_start(dst, src, **kw)
            return S.emit(eng, fn, reads=reads, writes=writes, sem=sem_name,
                          inc_n=16)

        dma("sync", "ms", iden[:], iden_d)
        dma("sync", "ms",
            wrg[:].rearrange("p (kc e) -> p kc e", e=16)[:, :, 0:8],
            wr_d.rearrange("(kc p) e -> p kc e", p=128))
        dma("sync", "ms",
            wrg[:].rearrange("p (kc e) -> p kc e", e=16)[:, :, 8:16],
            wg8_d.rearrange("(kc p) e -> p kc e", p=128))
        def slow_wg(e):
            return e.dma_start(wgcol[:], wgv_d[0, :].rearrange("(kc p) -> p kc", p=128),
                               allow_slow_non_contiguous=True)
        S.emit("sync", slow_wg, sem="ms", inc_n=16)
        dma("sync", "ms", bgv[:], bg_d.rearrange("(a b) -> a b", a=1))
        dma("sync", "ms", boutr[:], bout_d.rearrange("(a b) -> a b", a=1))
        for ec in range(8):
            dma("sync", "ms", oacc[ec][:], wout_d[ec*128:(ec+1)*128, :],
                writes=[r_oacc[ec]])
        EV_MS = ("ms", S.cnt["ms"], None)
        ms_events = [EV_MS]
        wout_stage_ev = [EV_MS] * 8
        for ec in range(8):
            r_oacc[ec].ready = EV_MS
        r_const.ready = EV_MS

        for c in range(min(NXSLOT, NCHUNK)):
            dma("sync", f"xd{c}", xr[c][:], x_d[c*128:(c+1)*128, :],
                writes=[r_xr[c]])
            fills[f"xd{c}"] += 1

        S.emit("vector", lambda e: e.memset(ones1[:], 1.0), writes=[r_const], sem="dve")
        S.emit("vector", lambda e: e.memset(onesr[:], 1.0), writes=[r_const], sem="dve")
        S.emit("vector", lambda e: e.memset(hstate[:], 0.0), writes=[r_hst], sem="dve")
        S.emit("vector", lambda e: e.memset(epsc[:], EPS), writes=[r_const], sem="dve")
        S.emit("vector",
               lambda e: e.tensor_scalar(nbg[:], bgv[:], -1.0, None, ALU.mult),
               extra=ms_events, writes=[r_const], sem="dve")

        for ec in range(8):
            for half in range(2):
                bank = r_TB[4 + half]
                def tfn(e, ec=ec, half=half):
                    last = None
                    for j in range(4):
                        dc = half * 4 + j
                        last = e.transpose(
                            TB[4 + half][:, j*128:(j+1)*128],
                            oacc[ec][:, dc*128:(dc+1)*128], iden[:])
                    return last
                S.emit("tensor", tfn, reads=[r_oacc[ec], r_const],
                       writes=[bank], extra=[wout_stage_ev[ec]] + ms_events,
                       sem="pe")
                def cfn(e, ec=ec, half=half):
                    last = None
                    for j in range(4):
                        dc = half * 4 + j
                        last = e.copy(woutT[:, dc*1024 + ec*128:
                                            dc*1024 + ec*128 + 128],
                                      TB[4 + half][:, j*128:(j+1)*128])
                    return last
                S.emit("scalar", cfn, reads=[bank], writes=[r_woutT], sem="act")

        def wg_src(n):
            idx = n % 128
            ee, s = idx // 16, idx % 16
            return wgate_d[ee, :, s*256:(s+1)*256].rearrange(
                "(kc p) f -> p kc f", p=128)
        def wu_src(n):
            idx = n % 128
            ee, s = idx // 16, idx % 16
            return wup_d[ee, :, s*256:(s+1)*256].rearrange(
                "(kc p) f -> p kc f", p=128)
        def wd_src(n):
            idx = n % 64
            ee, s8 = idx // 8, idx % 8
            return wdown_d[ee, s8*512:(s8+1)*512, :].rearrange(
                "(fc p) d -> p fc d", p=128)

        def emit_wg(n):
            sl = n % 2
            dma("gpsimd", f"wgs{sl}",
                wgr[sl][:].rearrange("p (kc f) -> p kc f", kc=8), wg_src(n),
                writes=[r_wgr[sl]])
            fills["wgs"] += 1
        def emit_wu(n):
            sl = n % 2
            dma("gpsimd", f"wus{sl}",
                wur[sl][:].rearrange("p (kc f) -> p kc f", kc=8), wu_src(n),
                writes=[r_wur[sl]])
            fills["wus"] += 1
        def emit_wd(n):
            sl = n % 2
            dma("gpsimd", f"wds{sl}",
                wdr[sl][:].rearrange("p (fc d) -> p fc d", fc=4), wd_src(n),
                writes=[r_wdr[sl]])
            fills["wds"] += 1

        emit_wg(0); emit_wu(0); emit_wd(0)
        emit_wg(1); emit_wu(1); emit_wd(1)
        nwg, nwu, nwd = 2, 2, 2

        def layer_norm(rslot, out_ap, out_res, read_res):
            rr = r_rsb[rslot]
            S.emit("vector",
                   lambda e: e.reduce_sum(cols["scol"][:], rsb[rslot][:], axis=AX.X),
                   reads=[rr], writes=[r_cols["scol"]], sem="dve")
            S.emit("vector",
                   lambda e: e.scalar_tensor_tensor(
                       x1r[:], rsb[rslot][:], 1.0, rsb[rslot][:],
                       ALU.mult, ALU.mult, accum_out=cols["qcol"][:]),
                   reads=[rr], writes=[r_x1r, r_cols["qcol"]], sem="dve")
            S.emit("vector",
                   lambda e: e.tensor_scalar(cols["mcol"][:], cols["scol"][:],
                                             1.0 / D, None, ALU.mult),
                   reads=[r_cols["scol"]], writes=[r_cols["mcol"]], sem="dve")
            S.emit("vector",
                   lambda e: e.tensor_tensor(cols["m2"][:], cols["mcol"][:],
                                             cols["mcol"][:], ALU.mult),
                   reads=[r_cols["mcol"]], writes=[r_cols["m2"]], sem="dve")
            S.emit("vector",
                   lambda e: e.scalar_tensor_tensor(
                       cols["vcol"][:], cols["qcol"][:], 1.0 / D, cols["m2"][:],
                       ALU.mult, ALU.subtract),
                   reads=[r_cols["qcol"], r_cols["m2"]],
                   writes=[r_cols["vcol"]], sem="dve")
            S.emit("scalar",
                   lambda e: e.activation(cols["lv"][:], cols["vcol"][:],
                                          AF.Ln, bias=epsc[:]),
                   reads=[r_cols["vcol"], r_const], writes=[r_cols["lv"]], sem="act")
            S.emit("scalar",
                   lambda e: e.activation(cols["rstd"][:], cols["lv"][:],
                                          AF.Exp, scale=-0.5),
                   reads=[r_cols["lv"]], writes=[r_cols["rstd"]], sem="act")
            return S.emit("vector",
                          lambda e: e.tensor_scalar(
                              out_ap, rsb[rslot][:], cols["mcol"][:],
                              cols["rstd"][:], ALU.subtract, ALU.mult),
                          reads=[rr, r_cols["mcol"], r_cols["rstd"]] + read_res,
                          writes=[out_res], sem="dve")

        for p in range(NPASS):
            for tg in range(2):
                for tc4 in range(4):
                    tc = tg * 4 + tc4
                    c = p * 8 + tc
                    slot = c % NXSLOT
                    for half in range(2):
                        bank = r_TB[half]
                        def tfn(e, slot=slot, half=half):
                            last = None
                            for j in range(4):
                                d = half * 4 + j
                                last = e.transpose(
                                    TB[half][:, j*128:(j+1)*128],
                                    xr[slot][:, d*128:(d+1)*128], iden[:])
                            return last
                        S.emit("tensor", tfn, reads=[r_xr[slot], r_const],
                               writes=[bank], sem="pe")
                        def efn(e, half=half, tc4=tc4):
                            last = None
                            for j in range(4):
                                d = half * 4 + j
                                last = e.tensor_copy(
                                    xt[d][:, tc4*128:(tc4+1)*128],
                                    TB[half][:, j*128:(j+1)*128])
                            return last
                        S.emit("vector", efn, reads=[bank],
                               writes=[r_xt[half*4], r_xt[half*4+1],
                                       r_xt[half*4+2], r_xt[half*4+3]],
                               sem="dve")
                def zfn(e):
                    last = None
                    for k in range(8):
                        last = e.matmul(TZ, wgcol[:, k:k+1], xt[k][:],
                                        start=(k == 0), stop=(k == 7))
                    return last
                S.emit("tensor", zfn, reads=[r_const] + r_xt, writes=[r_TZ],
                       sem="pe")
                S.emit("scalar",
                       lambda e: e.activation(erow[:], TZ, AF.Exp,
                                              bias=nbg[:], scale=-1.0),
                       reads=[r_TZ, r_const], writes=[r_rows], sem="act")
                S.emit("vector",
                       lambda e: e.tensor_scalar(grow[:], erow[:], 1.0, None,
                                                 ALU.add),
                       reads=[r_rows], writes=[r_rows], sem="dve")
                S.emit("vector", lambda e: e.reciprocal(grow[:], grow[:]),
                       reads=[r_rows], writes=[r_rows], sem="dve")
                S.emit("vector",
                       lambda e: e.tensor_scalar(arow[:], grow[:], -1.0, 1.0,
                                                 ALU.mult, ALU.add),
                       reads=[r_rows], writes=[r_rows], sem="dve")
                S.emit("tensor",
                       lambda e: e.matmul(TB[2][:], ones1[:], grow[:],
                                          start=True, stop=True),
                       reads=[r_rows, r_const], writes=[r_TB[2]], sem="pe")
                S.emit("tensor",
                       lambda e: e.matmul(TB[3][:], ones1[:], arow[:],
                                          start=True, stop=True),
                       reads=[r_rows, r_const], writes=[r_TB[3]], sem="pe")
                S.emit("vector", lambda e: e.tensor_copy(asb[:], TB[3][:]),
                       reads=[r_TB[3]], writes=[r_asb], sem="dve")
                for d in range(8):
                    par = d % 2
                    S.emit("vector",
                           lambda e, d=d, par=par: e.tensor_tensor(
                               bsb[par][:], xt[d][:], TB[2][:], ALU.mult),
                           reads=[r_xt[d], r_TB[2]], writes=[r_bsb[par]],
                           sem="dve")
                    S.emit("vector",
                           lambda e, d=d, par=par: e.tensor_tensor_scan(
                               scano[par][:], asb[:], bsb[par][:],
                               hstate[:, d:d+1], ALU.mult, ALU.add),
                           reads=[r_asb, r_bsb[par], r_hst],
                           writes=[r_scano[par]], sem="dve")
                    S.emit("vector",
                           lambda e, d=d, par=par: e.tensor_copy(
                               hstate[:, d:d+1], scano[par][:, 511:512]),
                           reads=[r_scano[par]], writes=[r_hst], sem="dve")
                    S.emit("scalar",
                           lambda e, d=d, par=par, tg=tg: e.copy(
                               htp[d][:, tg*512:(tg+1)*512], scano[par][:]),
                           reads=[r_scano[par]], writes=[r_htp[d]], sem="act")

            def emit_attn_mm(tc):
                for eh in range(2):
                    bank = r_TB[eh]
                    def afn(e, tc=tc, eh=eh):
                        for dc in range(8):
                            e.matmul(TB[eh][:],
                                     htp[dc][:, tc*128:(tc+1)*128],
                                     woutT[:, dc*1024 + eh*512:
                                           dc*1024 + eh*512 + 512],
                                     start=(dc == 0), stop=False)
                        return e.matmul(TB[eh][:], ones1[:],
                                        boutr[:, eh*512:eh*512 + 512],
                                        start=False, stop=True)
                    S.emit("tensor", afn,
                           reads=r_htp + [r_woutT, r_const],
                           writes=[bank], sem="pe")

            def emit_radds(tc):
                slot = (p * 8 + tc) % NXSLOT
                rslot = tc % 2
                for eh in range(2):
                    bank = r_TB[eh]
                    S.emit("vector",
                           lambda e, eh=eh, rslot=rslot, slot=slot:
                           e.tensor_tensor(rsb[rslot][:, eh*512:eh*512+512],
                                           TB[eh][:],
                                           xr[slot][:, eh*512:eh*512+512],
                                           ALU.add),
                           reads=[bank, r_xr[slot]], writes=[r_rsb[rslot]],
                           sem="dve")

            emit_attn_mm(0)
            for tc in range(8):
                c = p * 8 + tc
                slot = c % NXSLOT
                rslot = tc % 2
                emit_radds(tc)
                if tc + 1 < 8:
                    emit_attn_mm(tc + 1)
                layer_norm(rslot, xr[slot][:], r_xr[slot], [])
                for half in range(2):
                    bank = r_TB[4 + half]
                    def tfn(e, slot=slot, half=half):
                        last = None
                        for j in range(4):
                            k = half * 4 + j
                            last = e.transpose(
                                TB[4 + half][:, j*128:(j+1)*128],
                                xr[slot][:, k*128:(k+1)*128], iden[:])
                        return last
                    S.emit("tensor", tfn, reads=[r_xr[slot], r_const],
                           writes=[bank], sem="pe")
                    def efn(e, half=half):
                        last = None
                        for j in range(4):
                            k = half * 4 + j
                            last = e.tensor_copy(
                                x1r[:, k*128:(k+1)*128],
                                TB[4 + half][:, j*128:(j+1)*128])
                        return last
                    evac_ev = S.emit("vector", efn, reads=[bank],
                                     writes=[r_x1r], sem="dve")
                    def cfn(e, half=half, tc=tc):
                        last = None
                        for j in range(4):
                            k = half * 4 + j
                            last = e.copy(
                                x1f[k][:, tc*128:(tc+1)*128],
                                TB[4 + half][:, j*128:(j+1)*128])
                        return last
                    S.emit("scalar", cfn, reads=[bank], extra=[evac_ev],
                           writes=[r_x1f[half*4], r_x1f[half*4+1],
                                   r_x1f[half*4+2], r_x1f[half*4+3]],
                           sem="act")
                def rfn(e):
                    last = None
                    for k in range(8):
                        last = e.matmul(TR[:, 0:16], x1r[:, k*128:(k+1)*128],
                                        wrg[:, k*16:(k+1)*16],
                                        start=(k == 0), stop=(k == 7))
                    return last
                S.emit("tensor", rfn, reads=[r_x1r, r_const], writes=[r_TR],
                       sem="pe")
                S.emit("vector",
                       lambda e: e.reduce_max(cols["zmax"][:], TR[:, 0:8],
                                              axis=AX.X),
                       reads=[r_TR], writes=[r_cols["zmax"]], sem="dve")
                S.emit("vector",
                       lambda e: e.tensor_scalar(p8["prs"][:], TR[:, 0:8],
                                                 cols["zmax"][:], None,
                                                 ALU.subtract),
                       reads=[r_TR, r_cols["zmax"]], writes=[r_p8["prs"]],
                       sem="dve")
                S.emit("scalar",
                       lambda e: e.activation(p8["pre"][:], p8["prs"][:], AF.Exp),
                       reads=[r_p8["prs"]], writes=[r_p8["pre"]], sem="act")
                S.emit("vector",
                       lambda e: e.reduce_sum(cols["rsum"][:], p8["pre"][:],
                                              axis=AX.X),
                       reads=[r_p8["pre"]], writes=[r_cols["rsum"]], sem="dve")
                S.emit("vector",
                       lambda e: e.reciprocal(cols["rsum"][:], cols["rsum"][:]),
                       reads=[r_cols["rsum"]], writes=[r_cols["rsum"]], sem="dve")
                S.emit("vector",
                       lambda e: e.tensor_scalar(p8["prf"][:], p8["pre"][:],
                                                 cols["rsum"][:], None, ALU.mult),
                       reads=[r_p8["pre"], r_cols["rsum"]],
                       writes=[r_p8["prf"]], sem="dve")
                S.emit("vector",
                       lambda e: e.reduce_max(cols["zmax2"][:], TR[:, 8:16],
                                              axis=AX.X),
                       reads=[r_TR], writes=[r_cols["zmax2"]], sem="dve")
                S.emit("vector",
                       lambda e: e.tensor_scalar(p8["pgs"][:], TR[:, 8:16],
                                                 cols["zmax2"][:], None,
                                                 ALU.subtract),
                       reads=[r_TR, r_cols["zmax2"]], writes=[r_p8["pgs"]],
                       sem="dve")
                S.emit("scalar",
                       lambda e: e.activation(p8["pge"][:], p8["pgs"][:],
                                              AF.Exp, scale=ITEMP),
                       reads=[r_p8["pgs"]], writes=[r_p8["pge"]], sem="act")
                S.emit("vector",
                       lambda e: e.reduce_sum(cols["gsum"][:], p8["pge"][:],
                                              axis=AX.X),
                       reads=[r_p8["pge"]], writes=[r_cols["gsum"]], sem="dve")
                S.emit("vector",
                       lambda e: e.reciprocal(cols["gsum"][:], cols["gsum"][:]),
                       reads=[r_cols["gsum"]], writes=[r_cols["gsum"]], sem="dve")
                S.emit("vector",
                       lambda e: e.tensor_scalar(p8["pgf"][:], p8["pge"][:],
                                                 cols["gsum"][:], None, ALU.mult),
                       reads=[r_p8["pge"], r_cols["gsum"]],
                       writes=[r_p8["pgf"]], sem="dve")
                S.emit("vector",
                       lambda e, tc=tc: e.tensor_tensor(
                           cw[:, tc*8:(tc+1)*8], p8["prf"][:], p8["pgf"][:],
                           ALU.mult),
                       reads=[r_p8["prf"], r_p8["pgf"]], writes=[r_cw],
                       sem="dve")
                S.emit("vector",
                       lambda e, tc=tc: e.reduce_sum(
                           cols["csum"][:], cw[:, tc*8:(tc+1)*8], axis=AX.X),
                       reads=[r_cw], writes=[r_cols["csum"]], sem="dve")
                S.emit("vector",
                       lambda e: e.tensor_scalar(cols["csum"][:], cols["csum"][:],
                                                 1e-9, None, ALU.add),
                       reads=[r_cols["csum"]], writes=[r_cols["csum"]], sem="dve")
                S.emit("vector",
                       lambda e: e.reciprocal(cols["crec"][:], cols["csum"][:]),
                       reads=[r_cols["csum"]], writes=[r_cols["crec"]], sem="dve")
                S.emit("vector",
                       lambda e, tc=tc: e.tensor_scalar(
                           cw[:, tc*8:(tc+1)*8], cw[:, tc*8:(tc+1)*8],
                           cols["crec"][:], None, ALU.mult),
                       reads=[r_cols["crec"]], writes=[r_cw], sem="dve")

            def emit_L(tc):
                c = p * 8 + tc
                slot = c % NXSLOT
                rslot = tc % 2
                for eh in range(2):
                    S.emit("vector",
                           lambda e, tc=tc, eh=eh, rslot=rslot, slot=slot:
                           e.tensor_tensor(
                               rsb[rslot][:, eh*512:eh*512+512],
                               oacc[tc][:, eh*512:eh*512+512],
                               xr[slot][:, eh*512:eh*512+512], ALU.add),
                           reads=[r_oacc[tc], r_xr[slot]],
                           writes=[r_rsb[rslot]], sem="dve")
                layer_norm(rslot, rsb[rslot][:], r_rsb[rslot], [])
                oyn = f"oy{rslot}"
                fills["oy"] += 1
                dma("sync", oyn, y_d[c*128:(c+1)*128, :], rsb[rslot][:],
                    reads=[r_rsb[rslot]])
                nxt = c + NXSLOT
                if nxt < NCHUNK:
                    dma("sync", f"xd{slot}", xr[slot][:],
                        x_d[nxt*128:(nxt+1)*128, :], writes=[r_xr[slot]])
                    fills[f"xd{slot}"] += 1

            for ee in range(8):
                for q in range(4):
                    for fcq in range(8):
                        s16 = q * 4 + fcq // 2
                        wslot = (p * 128 + ee * 16 + s16) % 2
                        f_in = fcq % 2
                        for tg in range(2):
                            gpar = (fcq * 2 + tg) % 2
                            gbank, ubank = r_TB[gpar], r_TB[2 + gpar]
                            def gfn(e, wslot=wslot, f_in=f_in, tg=tg, gpar=gpar):
                                last = None
                                for k in range(8):
                                    last = e.matmul(
                                        TB[gpar][:],
                                        wgr[wslot][:, k*256 + f_in*128:
                                                    k*256 + f_in*128 + 128],
                                        x1f[k][:, tg*512:(tg+1)*512],
                                        start=(k == 0), stop=(k == 7))
                                return last
                            S.emit("tensor", gfn,
                                   reads=[r_wgr[wslot]] + r_x1f,
                                   writes=[gbank], sem="pe")
                            def ufn(e, wslot=wslot, f_in=f_in, tg=tg, gpar=gpar):
                                last = None
                                for k in range(8):
                                    last = e.matmul(
                                        TB[2 + gpar][:],
                                        wur[wslot][:, k*256 + f_in*128:
                                                    k*256 + f_in*128 + 128],
                                        x1f[k][:, tg*512:(tg+1)*512],
                                        start=(k == 0), stop=(k == 7))
                                return last
                            S.emit("tensor", ufn,
                                   reads=[r_wur[wslot]] + r_x1f,
                                   writes=[ubank], sem="pe")
                            S.emit("scalar",
                                   lambda e, gpar=gpar: e.activation(
                                       sg[gpar][:], TB[gpar][:], AF.Silu),
                                   reads=[gbank], writes=[r_sg[gpar]],
                                   sem="act")
                            S.emit("vector",
                                   lambda e, gpar=gpar, fcq=fcq, tg=tg:
                                   e.tensor_tensor(
                                       htp[fcq][:, tg*512:(tg+1)*512],
                                       sg[gpar][:], TB[2 + gpar][:], ALU.mult),
                                   reads=[r_sg[gpar], ubank],
                                   writes=[r_htp[fcq]], sem="dve")
                        if fcq % 2 == 1:
                            if nwg < NPASS * 128:
                                emit_wg(nwg); nwg += 1
                            if nwu < NPASS * 128:
                                emit_wu(nwu); nwu += 1
                    first = (ee == 0 and q == 0)
                    last_eq = (ee == 7 and q == 3)
                    for tc in range(8):
                        for dh in range(2):
                            if last_eq:
                                dpar = 2 + (tc * 2 + dh) % 4
                            else:
                                dpar = 4 + (tc * 2 + dh) % 2
                            bank = r_TB[dpar]
                            def dfn(e, tc=tc, dh=dh, dpar=dpar):
                                last = None
                                for fcq in range(8):
                                    wsl = fcq // 4
                                    last = e.matmul(
                                        TB[dpar][:],
                                        htp[fcq][:, tc*128:(tc+1)*128],
                                        wdr[wsl][:, (fcq % 4)*1024 + dh*512:
                                                 (fcq % 4)*1024 + dh*512 + 512],
                                        start=(fcq == 0), stop=(fcq == 7))
                                return last
                            S.emit("tensor", dfn,
                                   reads=r_htp + [r_wdr[0], r_wdr[1]],
                                   writes=[bank], sem="pe")
                            if first:
                                S.emit("vector",
                                       lambda e, tc=tc, dh=dh, dpar=dpar, ee=ee:
                                       e.tensor_scalar(
                                           oacc[tc][:, dh*512:dh*512+512],
                                           TB[dpar][:],
                                           cw[:, tc*8+ee:tc*8+ee+1], None,
                                           ALU.mult),
                                       reads=[bank, r_cw],
                                       writes=[r_oacc[tc]], sem="dve")
                            else:
                                S.emit("vector",
                                       lambda e, tc=tc, dh=dh, dpar=dpar, ee=ee:
                                       e.scalar_tensor_tensor(
                                           oacc[tc][:, dh*512:dh*512+512],
                                           TB[dpar][:],
                                           cw[:, tc*8+ee:tc*8+ee+1],
                                           oacc[tc][:, dh*512:dh*512+512],
                                           ALU.mult, ALU.add),
                                       reads=[bank, r_cw],
                                       writes=[r_oacc[tc]], sem="dve")
                        if last_eq:
                            emit_L(tc)
                    if nwd < NPASS * 64:
                        emit_wd(nwd); nwd += 1
                    if nwd < NPASS * 64:
                        emit_wd(nwd); nwd += 1

        S.emit("sync", lambda e: e.nop(),
               extra=[("oy0", S.cnt["oy0"], None), ("oy1", S.cnt["oy1"], None)])

        block = T(nc.Block())

        global LAST_NOPS
        LAST_NOPS = len(S.ops)
        ops_r = S.ops if TRUNC is None else S.ops[:TRUNC]
        def make_body(eng_name):
            def body(eng):
                floors = {}
                for op_eng, fn, waits, sem, inc_n in ops_r:
                    if op_eng != eng_name:
                        continue
                    for s, v in waits:
                        if floors.get(s, -1) < v:
                            eng.wait_ge(sems[s], v)
                            floors[s] = v
                    instr = fn(eng)
                    if sem is not None:
                        if instr is None:
                            instr = eng.nop()
                        instr.then_inc(sems[sem], inc_n)
            return body

        block.sync(make_body("sync"))
        block.scalar(make_body("scalar"))
        block.vector(make_body("vector"))
        block.tensor(make_body("tensor"))
        block.gpsimd(make_body("gpsimd"))

    return nc


_NC_CACHE = None

def kernel(L0_attn_x, L0_attn_Wg, L0_attn_bg, L0_attn_Wout, L0_attn_bout,
           L0_ffn_w_router, L0_ffn_w_gating, L0_ffn_w_gate, L0_ffn_w_up,
           L0_ffn_w_down):
    global _NC_CACHE
    if _NC_CACHE is None:
        _NC_CACHE = build_nc()
    nc = _NC_CACHE

    x = np.ascontiguousarray(np.asarray(L0_attn_x, dtype=np.float32))
    base = {
        "wgv": np.ascontiguousarray(np.asarray(L0_attn_Wg, dtype=np.float32)),
        "bgs": np.ascontiguousarray(np.asarray(L0_attn_bg, dtype=np.float32)),
        "wout": np.ascontiguousarray(np.asarray(L0_attn_Wout, dtype=np.float32)),
        "bout": np.ascontiguousarray(np.asarray(L0_attn_bout, dtype=np.float32)),
        "wrouter": np.ascontiguousarray(np.asarray(L0_ffn_w_router, dtype=np.float32)),
        "wgating": np.ascontiguousarray(np.asarray(L0_ffn_w_gating, dtype=np.float32)),
        "wgate": np.ascontiguousarray(np.asarray(L0_ffn_w_gate, dtype=np.float32)),
        "wup": np.ascontiguousarray(np.asarray(L0_ffn_w_up, dtype=np.float32)),
        "wdown": np.ascontiguousarray(np.asarray(L0_ffn_w_down, dtype=np.float32)),
        "iden": np.eye(128, dtype=np.float32),
    }
    in_maps = [dict(base, x=np.ascontiguousarray(x[b])) for b in range(8)]
    res = run_bass_kernel_spmd(nc, in_maps, core_ids=list(range(8)))
    return np.stack([np.asarray(r["y"]) for r in res.results], axis=0)


# version 05 (final)
# speedup vs baseline: 1.1653x; 1.0589x over previous
import numpy as np
from contextlib import ExitStack

import concourse.bass as bass
import concourse.mybir as mybir
from concourse.bass_utils import run_bass_kernel_spmd

F32 = mybir.dt.float32
BF16 = mybir.dt.bfloat16
AF = mybir.ActivationFunctionType
ALU = mybir.AluOpType
AX = mybir.AxisListType

L, D, E, FF = 4096, 1024, 8, 4096
TRUNC = None
LAST_NOPS = None
NPASS = 4
NTC = 8
NCHUNK = NPASS * NTC
NKC = 8
NXSLOT = 9
EPS = 1e-5
ITEMP = 10.0

ENGS = ("sync", "scalar", "vector", "tensor", "gpsimd")


class Res:
    __slots__ = ("name", "ready", "readers")
    def __init__(self, name):
        self.name = name
        self.ready = None
        self.readers = {}

class Sched:
    def __init__(self):
        self.ops = []
        self.cnt = {}

    SEM_ENG = {"pe": "tensor", "dve": "vector", "act": "scalar"}

    def emit(self, eng, fn, reads=(), writes=(), extra=(), sem=None, inc_n=1):
        waits = {}
        def add(ev):
            if ev is None:
                return
            s, v, peng = ev
            if waits.get(s, -1) < v:
                waits[s] = v
        for r in reads:
            add(r.ready)
        for w in writes:
            add(w.ready)
            for s, v in w.readers.items():
                add((s, v, self.SEM_ENG.get(s)))
        for ev in extra:
            add(ev)
        self.ops.append((eng, fn, tuple(waits.items()), sem, inc_n))
        ev = None
        if sem is not None:
            self.cnt[sem] = self.cnt.get(sem, 0) + inc_n
            peng = eng if sem in ("pe", "dve", "act") else None
            ev = (sem, self.cnt[sem], peng)
        for r in reads:
            if ev is not None:
                s, v, _ = ev
                if r.readers.get(s, -1) < v:
                    r.readers[s] = v
        for w in writes:
            w.ready = ev
            w.readers = {}
        return ev


def build_nc():
    nc = bass.Bass("TRN2", target_bir_lowering=False)

    x_d = nc.dram_tensor("x", [L, D], F32, kind="ExternalInput").ap()
    wgv_d = nc.dram_tensor("wgv", [1, D], F32, kind="ExternalInput").ap()
    bg_d = nc.dram_tensor("bgs", [1], F32, kind="ExternalInput").ap()
    wout_d = nc.dram_tensor("wout", [D, D], F32, kind="ExternalInput").ap()
    bout_d = nc.dram_tensor("bout", [D], F32, kind="ExternalInput").ap()
    wr_d = nc.dram_tensor("wrouter", [D, E], F32, kind="ExternalInput").ap()
    wg8_d = nc.dram_tensor("wgating", [D, E], F32, kind="ExternalInput").ap()
    wgate_d = nc.dram_tensor("wgate", [E, D, FF], F32, kind="ExternalInput").ap()
    wup_d = nc.dram_tensor("wup", [E, D, FF], F32, kind="ExternalInput").ap()
    wdown_d = nc.dram_tensor("wdown", [E, FF, D], F32, kind="ExternalInput").ap()
    iden_d = nc.dram_tensor("iden", [128, 128], F32, kind="ExternalInput").ap()
    y_d = nc.dram_tensor("y", [L, D], F32, kind="ExternalOutput").ap()

    S = Sched()

    with ExitStack() as es:
        T = es.enter_context

        iden = T(nc.sbuf_tensor("iden_s", [128, 128], F32))
        ones1 = T(nc.sbuf_tensor("ones1", [1, 128], F32))
        onesr = T(nc.sbuf_tensor("onesr", [1, 512], F32))
        woutT = T(nc.sbuf_tensor("woutT", [128, 8192], BF16))
        wrg = T(nc.sbuf_tensor("wrg", [128, 128], F32))
        wgcol = T(nc.sbuf_tensor("wgcol", [128, 8], F32))
        bgv = T(nc.sbuf_tensor("bgv", [1, 1], F32))
        nbg = T(nc.sbuf_tensor("nbg", [1, 1], F32))
        boutr = T(nc.sbuf_tensor("boutr", [1, 1024], F32))
        hstate = T(nc.sbuf_tensor("hstate", [128, 8], F32))
        epsc = T(nc.sbuf_tensor("epsc", [128, 1], F32))
        xr = [T(nc.sbuf_tensor(f"xr{i}", [128, 1024], F32)) for i in range(NXSLOT)]
        xt = [T(nc.sbuf_tensor(f"xt{d}", [128, 512], F32)) for d in range(8)]
        htp = [T(nc.sbuf_tensor(f"htp{d}", [128, 1024], BF16)) for d in range(8)]
        x1f = [T(nc.sbuf_tensor(f"x1f{k}", [128, 1024], BF16)) for k in range(8)]
        x1r = T(nc.sbuf_tensor("x1r", [128, 1024], F32))
        asb = T(nc.sbuf_tensor("asb", [128, 512], F32))
        bsb = [T(nc.sbuf_tensor(f"bsb{i}", [128, 512], F32)) for i in range(2)]
        scano = [T(nc.sbuf_tensor(f"scano{i}", [128, 512], F32)) for i in range(2)]
        sg = [T(nc.sbuf_tensor(f"sg{i}", [128, 512], F32)) for i in range(2)]
        rsb = [T(nc.sbuf_tensor(f"rsb{i}", [128, 1024], F32)) for i in range(2)]
        oacc = [T(nc.sbuf_tensor(f"oacc{t}", [128, 1024], F32)) for t in range(8)]
        cw = T(nc.sbuf_tensor("cw_s", [128, 64], F32))
        erow = T(nc.sbuf_tensor("erow", [1, 512], F32))
        grow = T(nc.sbuf_tensor("grow", [1, 512], F32))
        arow = T(nc.sbuf_tensor("arow", [1, 512], F32))
        cols = {n: T(nc.sbuf_tensor(f"c_{n}", [128, 1], F32))
                for n in ("scol", "qcol", "mcol", "m2", "vcol", "lv", "rstd",
                          "zmax", "rsum", "zmax2", "gsum", "csum", "crec")}
        p8 = {n: T(nc.sbuf_tensor(f"p8_{n}", [128, 8], F32))
              for n in ("prs", "pre", "prf", "pgs", "pge", "pgf")}
        wgr = [T(nc.sbuf_tensor(f"wgr{i}", [128, 2048], BF16)) for i in range(2)]
        wur = [T(nc.sbuf_tensor(f"wur{i}", [128, 2048], BF16)) for i in range(2)]
        wdr = [T(nc.sbuf_tensor(f"wdr{i}", [128, 4096], BF16)) for i in range(2)]

        TB = [T(nc.psum_tensor(f"tb{i}", [128, 512], F32)) for i in range(6)]
        TZf = T(nc.psum_tensor("tz", [128, 512], F32))
        TZ = TZf[0:1, :]
        TRf = T(nc.psum_tensor("tr", [128, 512], F32))
        TR = TRf[:, 0:16]

        sem_names = (["pe", "dve", "act", "ms"]
                     + [f"xd{i}" for i in range(NXSLOT)]
                     + ["wgs0", "wgs1", "wus0", "wus1", "wds0", "wds1",
                        "oy0", "oy1"])
        sems = {n: T(nc.semaphore(n)) for n in sem_names}

        R = {}
        def res(name):
            if name not in R:
                R[name] = Res(name)
            return R[name]

        r_xr = [res(f"xr{i}") for i in range(NXSLOT)]
        r_xt = [res(f"xt{d}") for d in range(8)]
        r_htp = [res(f"htp{d}") for d in range(8)]
        r_x1f = [res(f"x1f{k}") for k in range(8)]
        r_x1r = res("x1r")
        r_asb = res("asb")
        r_bsb = [res(f"bsb{i}") for i in range(2)]
        r_scano = [res(f"scano{i}") for i in range(2)]
        r_sg = [res(f"sg{i}") for i in range(2)]
        r_rsb = [res(f"rsb{i}") for i in range(2)]
        r_oacc = [res(f"oacc{t}") for t in range(8)]
        r_cw = res("cw")
        r_rows = res("rows")
        r_cols = {n: res(f"c_{n}") for n in cols}
        r_p8 = {n: res(f"p8_{n}") for n in p8}
        r_TB = [res(f"tb{i}") for i in range(6)]
        r_TZ = res("tz")
        r_TR = res("tr")
        r_wgr = [res(f"wgr{i}") for i in range(2)]
        r_wur = [res(f"wur{i}") for i in range(2)]
        r_wdr = [res(f"wdr{i}") for i in range(2)]
        r_hst = res("hstate")
        r_const = res("const")
        r_woutT = res("woutT")

        fills = {"wgs": 0, "wus": 0, "wds": 0, "oy": 0,
                 **{f"xd{i}": 0 for i in range(NXSLOT)}}

        def dma(eng, sem_name, dst, src, reads=(), writes=(), **kw):
            fn = lambda e, dst=dst, src=src, kw=kw: e.dma_start(dst, src, **kw)
            return S.emit(eng, fn, reads=reads, writes=writes, sem=sem_name,
                          inc_n=16)

        dma("sync", "ms", iden[:], iden_d)
        dma("sync", "ms",
            wrg[:].rearrange("p (kc e) -> p kc e", e=16)[:, :, 0:8],
            wr_d.rearrange("(kc p) e -> p kc e", p=128))
        dma("sync", "ms",
            wrg[:].rearrange("p (kc e) -> p kc e", e=16)[:, :, 8:16],
            wg8_d.rearrange("(kc p) e -> p kc e", p=128))
        def slow_wg(e):
            return e.dma_start(wgcol[:], wgv_d[0, :].rearrange("(kc p) -> p kc", p=128),
                               allow_slow_non_contiguous=True)
        S.emit("sync", slow_wg, sem="ms", inc_n=16)
        dma("sync", "ms", bgv[:], bg_d.rearrange("(a b) -> a b", a=1))
        dma("sync", "ms", boutr[:], bout_d.rearrange("(a b) -> a b", a=1))
        for ec in range(8):
            dma("sync", "ms", oacc[ec][:], wout_d[ec*128:(ec+1)*128, :],
                writes=[r_oacc[ec]])
        EV_MS = ("ms", S.cnt["ms"], None)
        ms_events = [EV_MS]
        wout_stage_ev = [EV_MS] * 8
        for ec in range(8):
            r_oacc[ec].ready = EV_MS
        r_const.ready = EV_MS

        for c in range(min(NXSLOT, NCHUNK)):
            dma("sync", f"xd{c}", xr[c][:], x_d[c*128:(c+1)*128, :],
                writes=[r_xr[c]])
            fills[f"xd{c}"] += 1

        S.emit("vector", lambda e: e.memset(ones1[:], 1.0), writes=[r_const], sem="dve")
        S.emit("vector", lambda e: e.memset(onesr[:], 1.0), writes=[r_const], sem="dve")
        S.emit("vector", lambda e: e.memset(hstate[:], 0.0), writes=[r_hst], sem="dve")
        S.emit("vector", lambda e: e.memset(epsc[:], EPS), writes=[r_const], sem="dve")
        S.emit("vector",
               lambda e: e.tensor_scalar(nbg[:], bgv[:], -1.0, None, ALU.mult),
               extra=ms_events, writes=[r_const], sem="dve")

        for ec in range(8):
            for half in range(2):
                bank = r_TB[4 + half]
                def tfn(e, ec=ec, half=half):
                    last = None
                    for j in range(4):
                        dc = half * 4 + j
                        last = e.transpose(
                            TB[4 + half][:, j*128:(j+1)*128],
                            oacc[ec][:, dc*128:(dc+1)*128], iden[:])
                    return last
                S.emit("tensor", tfn, reads=[r_oacc[ec], r_const],
                       writes=[bank], extra=[wout_stage_ev[ec]] + ms_events,
                       sem="pe")
                def cfn(e, ec=ec, half=half):
                    last = None
                    for j in range(4):
                        dc = half * 4 + j
                        last = e.copy(woutT[:, dc*1024 + ec*128:
                                            dc*1024 + ec*128 + 128],
                                      TB[4 + half][:, j*128:(j+1)*128])
                    return last
                S.emit("scalar", cfn, reads=[bank], writes=[r_woutT], sem="act")

        def wg_src(n):
            idx = n % 128
            ee, s = idx // 16, idx % 16
            return wgate_d[ee, :, s*256:(s+1)*256].rearrange(
                "(kc p) f -> p kc f", p=128)
        def wu_src(n):
            idx = n % 128
            ee, s = idx // 16, idx % 16
            return wup_d[ee, :, s*256:(s+1)*256].rearrange(
                "(kc p) f -> p kc f", p=128)
        def wd_src(n):
            idx = n % 64
            ee, s8 = idx // 8, idx % 8
            return wdown_d[ee, s8*512:(s8+1)*512, :].rearrange(
                "(fc p) d -> p fc d", p=128)

        def emit_wg(n):
            sl = n % 2
            dma("gpsimd", f"wgs{sl}",
                wgr[sl][:].rearrange("p (kc f) -> p kc f", kc=8), wg_src(n),
                writes=[r_wgr[sl]])
            fills["wgs"] += 1
        def emit_wu(n):
            sl = n % 2
            dma("gpsimd", f"wus{sl}",
                wur[sl][:].rearrange("p (kc f) -> p kc f", kc=8), wu_src(n),
                writes=[r_wur[sl]])
            fills["wus"] += 1
        def emit_wd(n):
            sl = n % 2
            dma("gpsimd", f"wds{sl}",
                wdr[sl][:].rearrange("p (fc d) -> p fc d", fc=4), wd_src(n),
                writes=[r_wdr[sl]])
            fills["wds"] += 1

        emit_wg(0); emit_wu(0); emit_wd(0)
        emit_wg(1); emit_wu(1); emit_wd(1)
        nwg, nwu, nwd = 2, 2, 2

        def layer_norm(rslot, out_ap, out_res, read_res):
            rr = r_rsb[rslot]
            S.emit("vector",
                   lambda e: e.reduce_sum(cols["scol"][:], rsb[rslot][:], axis=AX.X),
                   reads=[rr], writes=[r_cols["scol"]], sem="dve")
            S.emit("vector",
                   lambda e: e.scalar_tensor_tensor(
                       x1r[:], rsb[rslot][:], 1.0, rsb[rslot][:],
                       ALU.mult, ALU.mult, accum_out=cols["qcol"][:]),
                   reads=[rr], writes=[r_x1r, r_cols["qcol"]], sem="dve")
            S.emit("vector",
                   lambda e: e.tensor_scalar(cols["mcol"][:], cols["scol"][:],
                                             1.0 / D, None, ALU.mult),
                   reads=[r_cols["scol"]], writes=[r_cols["mcol"]], sem="dve")
            S.emit("vector",
                   lambda e: e.tensor_tensor(cols["m2"][:], cols["mcol"][:],
                                             cols["mcol"][:], ALU.mult),
                   reads=[r_cols["mcol"]], writes=[r_cols["m2"]], sem="dve")
            S.emit("vector",
                   lambda e: e.scalar_tensor_tensor(
                       cols["vcol"][:], cols["qcol"][:], 1.0 / D, cols["m2"][:],
                       ALU.mult, ALU.subtract),
                   reads=[r_cols["qcol"], r_cols["m2"]],
                   writes=[r_cols["vcol"]], sem="dve")
            S.emit("scalar",
                   lambda e: e.activation(cols["lv"][:], cols["vcol"][:],
                                          AF.Ln, bias=epsc[:]),
                   reads=[r_cols["vcol"], r_const], writes=[r_cols["lv"]], sem="act")
            S.emit("scalar",
                   lambda e: e.activation(cols["rstd"][:], cols["lv"][:],
                                          AF.Exp, scale=-0.5),
                   reads=[r_cols["lv"]], writes=[r_cols["rstd"]], sem="act")
            return S.emit("vector",
                          lambda e: e.tensor_scalar(
                              out_ap, rsb[rslot][:], cols["mcol"][:],
                              cols["rstd"][:], ALU.subtract, ALU.mult),
                          reads=[rr, r_cols["mcol"], r_cols["rstd"]] + read_res,
                          writes=[out_res], sem="dve")

        for p in range(NPASS):
            def emit_S(tg):
                for tc4 in range(4):
                    tc = tg * 4 + tc4
                    c = p * 8 + tc
                    slot = c % NXSLOT
                    for half in range(2):
                        bank = r_TB[2 + half]
                        def tfn(e, slot=slot, half=half):
                            last = None
                            for j in range(4):
                                d = half * 4 + j
                                last = e.transpose(
                                    TB[2 + half][:, j*128:(j+1)*128],
                                    xr[slot][:, d*128:(d+1)*128], iden[:])
                            return last
                        S.emit("tensor", tfn, reads=[r_xr[slot], r_const],
                               writes=[bank], sem="pe")
                        def efn(e, half=half, tc4=tc4):
                            last = None
                            for j in range(4):
                                d = half * 4 + j
                                last = e.tensor_copy(
                                    xt[d][:, tc4*128:(tc4+1)*128],
                                    TB[2 + half][:, j*128:(j+1)*128])
                            return last
                        S.emit("vector", efn, reads=[bank],
                               writes=[r_xt[half*4], r_xt[half*4+1],
                                       r_xt[half*4+2], r_xt[half*4+3]],
                               sem="dve")
                def zfn(e):
                    last = None
                    for k in range(8):
                        last = e.matmul(TZ, wgcol[:, k:k+1], xt[k][:],
                                        start=(k == 0), stop=(k == 7))
                    return last
                S.emit("tensor", zfn, reads=[r_const] + r_xt, writes=[r_TZ],
                       sem="pe")
                S.emit("scalar",
                       lambda e: e.activation(erow[:], TZ, AF.Exp,
                                              bias=nbg[:], scale=-1.0),
                       reads=[r_TZ, r_const], writes=[r_rows], sem="act")
                S.emit("vector",
                       lambda e: e.tensor_scalar(grow[:], erow[:], 1.0, None,
                                                 ALU.add),
                       reads=[r_rows], writes=[r_rows], sem="dve")
                S.emit("vector", lambda e: e.reciprocal(grow[:], grow[:]),
                       reads=[r_rows], writes=[r_rows], sem="dve")
                S.emit("vector",
                       lambda e: e.tensor_scalar(arow[:], grow[:], -1.0, 1.0,
                                                 ALU.mult, ALU.add),
                       reads=[r_rows], writes=[r_rows], sem="dve")
                S.emit("tensor",
                       lambda e: e.matmul(TB[2][:], ones1[:], grow[:],
                                          start=True, stop=True),
                       reads=[r_rows, r_const], writes=[r_TB[2]], sem="pe")
                S.emit("tensor",
                       lambda e: e.matmul(TB[3][:], ones1[:], arow[:],
                                          start=True, stop=True),
                       reads=[r_rows, r_const], writes=[r_TB[3]], sem="pe")
                S.emit("vector", lambda e: e.tensor_copy(asb[:], TB[3][:]),
                       reads=[r_TB[3]], writes=[r_asb], sem="dve")
                for d in range(8):
                    par = d % 2
                    S.emit("vector",
                           lambda e, d=d, par=par: e.tensor_tensor(
                               bsb[par][:], xt[d][:], TB[2][:], ALU.mult),
                           reads=[r_xt[d], r_TB[2]], writes=[r_bsb[par]],
                           sem="dve")
                    S.emit("vector",
                           lambda e, d=d, par=par: e.tensor_tensor_scan(
                               scano[par][:], asb[:], bsb[par][:],
                               hstate[:, d:d+1], ALU.mult, ALU.add),
                           reads=[r_asb, r_bsb[par], r_hst],
                           writes=[r_scano[par]], sem="dve")
                    S.emit("vector",
                           lambda e, d=d, par=par: e.tensor_copy(
                               hstate[:, d:d+1], scano[par][:, 511:512]),
                           reads=[r_scano[par]], writes=[r_hst], sem="dve")
                    S.emit("scalar",
                           lambda e, d=d, par=par, tg=tg: e.copy(
                               htp[d][:, tg*512:(tg+1)*512], scano[par][:]),
                           reads=[r_scano[par]], writes=[r_htp[d]], sem="act")

            ATTN_BANKS = [0, 1, 4, 5]
            def emit_attn_mm(tc):
                for eh in range(2):
                    bi = ATTN_BANKS[(tc * 2 + eh) % 4]
                    bank = r_TB[bi]
                    def afn(e, tc=tc, eh=eh, bi=bi):
                        for dc in range(8):
                            e.matmul(TB[bi][:],
                                     htp[dc][:, tc*128:(tc+1)*128],
                                     woutT[:, dc*1024 + eh*512:
                                           dc*1024 + eh*512 + 512],
                                     start=(dc == 0), stop=False)
                        return e.matmul(TB[bi][:], ones1[:],
                                        boutr[:, eh*512:eh*512 + 512],
                                        start=False, stop=True)
                    S.emit("tensor", afn,
                           reads=r_htp + [r_woutT, r_const],
                           writes=[bank], sem="pe")

            def emit_radds(tc):
                slot = (p * 8 + tc) % NXSLOT
                rslot = tc % 2
                for eh in range(2):
                    bi = ATTN_BANKS[(tc * 2 + eh) % 4]
                    bank = r_TB[bi]
                    S.emit("vector",
                           lambda e, eh=eh, rslot=rslot, slot=slot, bi=bi:
                           e.tensor_tensor(rsb[rslot][:, eh*512:eh*512+512],
                                           TB[bi][:],
                                           xr[slot][:, eh*512:eh*512+512],
                                           ALU.add),
                           reads=[bank, r_xr[slot]], writes=[r_rsb[rslot]],
                           sem="dve")

            emit_S(0)
            emit_attn_mm(0)
            emit_attn_mm(1)
            emit_S(1)
            for tc in range(8):
                c = p * 8 + tc
                slot = c % NXSLOT
                rslot = tc % 2
                emit_radds(tc)
                if tc + 2 < 8:
                    emit_attn_mm(tc + 2)
                layer_norm(rslot, xr[slot][:], r_xr[slot], [])
                for half in range(2):
                    tbank = TZf if half == 0 else TRf
                    bank = r_TZ if half == 0 else r_TR
                    def tfn(e, slot=slot, half=half, tbank=tbank):
                        last = None
                        for j in range(4):
                            k = half * 4 + j
                            last = e.transpose(
                                tbank[:, j*128:(j+1)*128],
                                xr[slot][:, k*128:(k+1)*128], iden[:])
                        return last
                    S.emit("tensor", tfn, reads=[r_xr[slot], r_const],
                           writes=[bank], sem="pe")
                    def efn(e, half=half, tbank=tbank):
                        last = None
                        for j in range(4):
                            k = half * 4 + j
                            last = e.tensor_copy(
                                x1r[:, k*128:(k+1)*128],
                                tbank[:, j*128:(j+1)*128])
                        return last
                    evac_ev = S.emit("vector", efn, reads=[bank],
                                     writes=[r_x1r], sem="dve")
                    def cfn(e, half=half, tc=tc, tbank=tbank):
                        last = None
                        for j in range(4):
                            k = half * 4 + j
                            last = e.copy(
                                x1f[k][:, tc*128:(tc+1)*128],
                                tbank[:, j*128:(j+1)*128])
                        return last
                    S.emit("scalar", cfn, reads=[bank], extra=[evac_ev],
                           writes=[r_x1f[half*4], r_x1f[half*4+1],
                                   r_x1f[half*4+2], r_x1f[half*4+3]],
                           sem="act")
                def rfn(e):
                    last = None
                    for k in range(8):
                        last = e.matmul(TR[:, 0:16], x1r[:, k*128:(k+1)*128],
                                        wrg[:, k*16:(k+1)*16],
                                        start=(k == 0), stop=(k == 7))
                    return last
                S.emit("tensor", rfn, reads=[r_x1r, r_const], writes=[r_TR],
                       sem="pe")
                S.emit("vector",
                       lambda e: e.reduce_max(cols["zmax"][:], TR[:, 0:8],
                                              axis=AX.X),
                       reads=[r_TR], writes=[r_cols["zmax"]], sem="dve")
                S.emit("vector",
                       lambda e: e.tensor_scalar(p8["prs"][:], TR[:, 0:8],
                                                 cols["zmax"][:], None,
                                                 ALU.subtract),
                       reads=[r_TR, r_cols["zmax"]], writes=[r_p8["prs"]],
                       sem="dve")
                S.emit("scalar",
                       lambda e: e.activation(p8["pre"][:], p8["prs"][:], AF.Exp),
                       reads=[r_p8["prs"]], writes=[r_p8["pre"]], sem="act")
                S.emit("vector",
                       lambda e: e.reduce_sum(cols["rsum"][:], p8["pre"][:],
                                              axis=AX.X),
                       reads=[r_p8["pre"]], writes=[r_cols["rsum"]], sem="dve")
                S.emit("vector",
                       lambda e: e.reciprocal(cols["rsum"][:], cols["rsum"][:]),
                       reads=[r_cols["rsum"]], writes=[r_cols["rsum"]], sem="dve")
                S.emit("vector",
                       lambda e: e.tensor_scalar(p8["prf"][:], p8["pre"][:],
                                                 cols["rsum"][:], None, ALU.mult),
                       reads=[r_p8["pre"], r_cols["rsum"]],
                       writes=[r_p8["prf"]], sem="dve")
                S.emit("vector",
                       lambda e: e.reduce_max(cols["zmax2"][:], TR[:, 8:16],
                                              axis=AX.X),
                       reads=[r_TR], writes=[r_cols["zmax2"]], sem="dve")
                S.emit("vector",
                       lambda e: e.tensor_scalar(p8["pgs"][:], TR[:, 8:16],
                                                 cols["zmax2"][:], None,
                                                 ALU.subtract),
                       reads=[r_TR, r_cols["zmax2"]], writes=[r_p8["pgs"]],
                       sem="dve")
                S.emit("scalar",
                       lambda e: e.activation(p8["pge"][:], p8["pgs"][:],
                                              AF.Exp, scale=ITEMP),
                       reads=[r_p8["pgs"]], writes=[r_p8["pge"]], sem="act")
                S.emit("vector",
                       lambda e: e.reduce_sum(cols["gsum"][:], p8["pge"][:],
                                              axis=AX.X),
                       reads=[r_p8["pge"]], writes=[r_cols["gsum"]], sem="dve")
                S.emit("vector",
                       lambda e: e.reciprocal(cols["gsum"][:], cols["gsum"][:]),
                       reads=[r_cols["gsum"]], writes=[r_cols["gsum"]], sem="dve")
                S.emit("vector",
                       lambda e: e.tensor_scalar(p8["pgf"][:], p8["pge"][:],
                                                 cols["gsum"][:], None, ALU.mult),
                       reads=[r_p8["pge"], r_cols["gsum"]],
                       writes=[r_p8["pgf"]], sem="dve")
                S.emit("vector",
                       lambda e, tc=tc: e.tensor_tensor(
                           cw[:, tc*8:(tc+1)*8], p8["prf"][:], p8["pgf"][:],
                           ALU.mult),
                       reads=[r_p8["prf"], r_p8["pgf"]], writes=[r_cw],
                       sem="dve")
                S.emit("vector",
                       lambda e, tc=tc: e.reduce_sum(
                           cols["csum"][:], cw[:, tc*8:(tc+1)*8], axis=AX.X),
                       reads=[r_cw], writes=[r_cols["csum"]], sem="dve")
                S.emit("vector",
                       lambda e: e.tensor_scalar(cols["csum"][:], cols["csum"][:],
                                                 1e-9, None, ALU.add),
                       reads=[r_cols["csum"]], writes=[r_cols["csum"]], sem="dve")
                S.emit("vector",
                       lambda e: e.reciprocal(cols["crec"][:], cols["csum"][:]),
                       reads=[r_cols["csum"]], writes=[r_cols["crec"]], sem="dve")
                S.emit("vector",
                       lambda e, tc=tc: e.tensor_scalar(
                           cw[:, tc*8:(tc+1)*8], cw[:, tc*8:(tc+1)*8],
                           cols["crec"][:], None, ALU.mult),
                       reads=[r_cols["crec"]], writes=[r_cw], sem="dve")

            def emit_L(tc):
                c = p * 8 + tc
                slot = c % NXSLOT
                rslot = tc % 2
                for eh in range(2):
                    S.emit("vector",
                           lambda e, tc=tc, eh=eh, rslot=rslot, slot=slot:
                           e.tensor_tensor(
                               rsb[rslot][:, eh*512:eh*512+512],
                               oacc[tc][:, eh*512:eh*512+512],
                               xr[slot][:, eh*512:eh*512+512], ALU.add),
                           reads=[r_oacc[tc], r_xr[slot]],
                           writes=[r_rsb[rslot]], sem="dve")
                layer_norm(rslot, rsb[rslot][:], r_rsb[rslot], [])
                oyn = f"oy{rslot}"
                fills["oy"] += 1
                dma("sync", oyn, y_d[c*128:(c+1)*128, :], rsb[rslot][:],
                    reads=[r_rsb[rslot]])
                nxt = c + NXSLOT
                if nxt < NCHUNK:
                    dma("sync", f"xd{slot}", xr[slot][:],
                        x_d[nxt*128:(nxt+1)*128, :], writes=[r_xr[slot]])
                    fills[f"xd{slot}"] += 1

            for ee in range(8):
                for q in range(4):
                    for fcq in range(8):
                        s16 = q * 4 + fcq // 2
                        wslot = (p * 128 + ee * 16 + s16) % 2
                        f_in = fcq % 2
                        for tg in range(2):
                            gpar = (fcq * 2 + tg) % 2
                            gbank, ubank = r_TB[gpar], r_TB[2 + gpar]
                            def gfn(e, wslot=wslot, f_in=f_in, tg=tg, gpar=gpar):
                                last = None
                                for k in range(8):
                                    last = e.matmul(
                                        TB[gpar][:],
                                        wgr[wslot][:, k*256 + f_in*128:
                                                    k*256 + f_in*128 + 128],
                                        x1f[k][:, tg*512:(tg+1)*512],
                                        start=(k == 0), stop=(k == 7))
                                return last
                            S.emit("tensor", gfn,
                                   reads=[r_wgr[wslot]] + r_x1f,
                                   writes=[gbank], sem="pe")
                            def ufn(e, wslot=wslot, f_in=f_in, tg=tg, gpar=gpar):
                                last = None
                                for k in range(8):
                                    last = e.matmul(
                                        TB[2 + gpar][:],
                                        wur[wslot][:, k*256 + f_in*128:
                                                    k*256 + f_in*128 + 128],
                                        x1f[k][:, tg*512:(tg+1)*512],
                                        start=(k == 0), stop=(k == 7))
                                return last
                            S.emit("tensor", ufn,
                                   reads=[r_wur[wslot]] + r_x1f,
                                   writes=[ubank], sem="pe")
                            S.emit("scalar",
                                   lambda e, gpar=gpar: e.activation(
                                       sg[gpar][:], TB[gpar][:], AF.Silu),
                                   reads=[gbank], writes=[r_sg[gpar]],
                                   sem="act")
                            S.emit("vector",
                                   lambda e, gpar=gpar, fcq=fcq, tg=tg:
                                   e.tensor_tensor(
                                       htp[fcq][:, tg*512:(tg+1)*512],
                                       sg[gpar][:], TB[2 + gpar][:], ALU.mult),
                                   reads=[r_sg[gpar], ubank],
                                   writes=[r_htp[fcq]], sem="dve")
                        if fcq % 2 == 1:
                            if nwg < NPASS * 128:
                                emit_wg(nwg); nwg += 1
                            if nwu < NPASS * 128:
                                emit_wu(nwu); nwu += 1
                    first = (ee == 0 and q == 0)
                    last_eq = (ee == 7 and q == 3)
                    for tc in range(8):
                        for dh in range(2):
                            if last_eq:
                                dpar = 2 + (tc * 2 + dh) % 4
                            else:
                                dpar = 4 + (tc * 2 + dh) % 2
                            bank = r_TB[dpar]
                            def dfn(e, tc=tc, dh=dh, dpar=dpar):
                                last = None
                                for fcq in range(8):
                                    wsl = fcq // 4
                                    last = e.matmul(
                                        TB[dpar][:],
                                        htp[fcq][:, tc*128:(tc+1)*128],
                                        wdr[wsl][:, (fcq % 4)*1024 + dh*512:
                                                 (fcq % 4)*1024 + dh*512 + 512],
                                        start=(fcq == 0), stop=(fcq == 7))
                                return last
                            S.emit("tensor", dfn,
                                   reads=r_htp + [r_wdr[0], r_wdr[1]],
                                   writes=[bank], sem="pe")
                            if first:
                                S.emit("vector",
                                       lambda e, tc=tc, dh=dh, dpar=dpar, ee=ee:
                                       e.tensor_scalar(
                                           oacc[tc][:, dh*512:dh*512+512],
                                           TB[dpar][:],
                                           cw[:, tc*8+ee:tc*8+ee+1], None,
                                           ALU.mult),
                                       reads=[bank, r_cw],
                                       writes=[r_oacc[tc]], sem="dve")
                            else:
                                S.emit("vector",
                                       lambda e, tc=tc, dh=dh, dpar=dpar, ee=ee:
                                       e.scalar_tensor_tensor(
                                           oacc[tc][:, dh*512:dh*512+512],
                                           TB[dpar][:],
                                           cw[:, tc*8+ee:tc*8+ee+1],
                                           oacc[tc][:, dh*512:dh*512+512],
                                           ALU.mult, ALU.add),
                                       reads=[bank, r_cw],
                                       writes=[r_oacc[tc]], sem="dve")
                        if last_eq:
                            emit_L(tc)
                    if nwd < NPASS * 64:
                        emit_wd(nwd); nwd += 1
                    if nwd < NPASS * 64:
                        emit_wd(nwd); nwd += 1

        S.emit("sync", lambda e: e.nop(),
               extra=[("oy0", S.cnt["oy0"], None), ("oy1", S.cnt["oy1"], None)])

        block = T(nc.Block())

        global LAST_NOPS
        LAST_NOPS = len(S.ops)
        ops_r = S.ops if TRUNC is None else S.ops[:TRUNC]
        def make_body(eng_name):
            def body(eng):
                floors = {}
                for op_eng, fn, waits, sem, inc_n in ops_r:
                    if op_eng != eng_name:
                        continue
                    for s, v in waits:
                        if floors.get(s, -1) < v:
                            eng.wait_ge(sems[s], v)
                            floors[s] = v
                    instr = fn(eng)
                    if sem is not None:
                        if instr is None:
                            instr = eng.nop()
                        instr.then_inc(sems[sem], inc_n)
            return body

        block.sync(make_body("sync"))
        block.scalar(make_body("scalar"))
        block.vector(make_body("vector"))
        block.tensor(make_body("tensor"))
        block.gpsimd(make_body("gpsimd"))

    return nc


_NC_CACHE = None

def kernel(L0_attn_x, L0_attn_Wg, L0_attn_bg, L0_attn_Wout, L0_attn_bout,
           L0_ffn_w_router, L0_ffn_w_gating, L0_ffn_w_gate, L0_ffn_w_up,
           L0_ffn_w_down):
    global _NC_CACHE
    if _NC_CACHE is None:
        _NC_CACHE = build_nc()
    nc = _NC_CACHE

    x = np.ascontiguousarray(np.asarray(L0_attn_x, dtype=np.float32))
    base = {
        "wgv": np.ascontiguousarray(np.asarray(L0_attn_Wg, dtype=np.float32)),
        "bgs": np.ascontiguousarray(np.asarray(L0_attn_bg, dtype=np.float32)),
        "wout": np.ascontiguousarray(np.asarray(L0_attn_Wout, dtype=np.float32)),
        "bout": np.ascontiguousarray(np.asarray(L0_attn_bout, dtype=np.float32)),
        "wrouter": np.ascontiguousarray(np.asarray(L0_ffn_w_router, dtype=np.float32)),
        "wgating": np.ascontiguousarray(np.asarray(L0_ffn_w_gating, dtype=np.float32)),
        "wgate": np.ascontiguousarray(np.asarray(L0_ffn_w_gate, dtype=np.float32)),
        "wup": np.ascontiguousarray(np.asarray(L0_ffn_w_up, dtype=np.float32)),
        "wdown": np.ascontiguousarray(np.asarray(L0_ffn_w_down, dtype=np.float32)),
        "iden": np.eye(128, dtype=np.float32),
    }
    in_maps = [dict(base, x=np.ascontiguousarray(x[b])) for b in range(8)]
    res = run_bass_kernel_spmd(nc, in_maps, core_ids=list(range(8)))
    return np.stack([np.asarray(r["y"]) for r in res.results], axis=0)
